# version 02
# baseline (speedup 1.0000x reference)
import math
import numpy as np
import ml_dtypes
import concourse.bass as bass
import concourse.mybir as mybir
from concourse.bass_utils import run_bass_kernel_spmd

F32 = mybir.dt.float32
BF16 = mybir.dt.bfloat16
F8 = mybir.dt.float8e4
AF = mybir.ActivationFunctionType
ALU = mybir.AluOpType
BF16NP = ml_dtypes.bfloat16

D = 2048
T = 2048
S = 4096
NH = 16
NC = 8
DC = D // 128
I0 = 6144
I1 = 8192
QAR = 1536
KVR = 512
QP = 64
EXP_BIAS = -25.0
NEG = -1.0e30
SBUF_TOP = 229344

N_SUB = 6


class B:
    def __init__(self, nc, ctx):
        self.nc = nc
        self.sem = {}
        self.val = {}
        for name in ("ld", "tm", "ac", "dv", "cc", "bar",
                     "sA", "sB", "sC", "sD", "sE", "sF", "sG", "sH"):
            self.sem[name] = ctx.enter_context(nc.semaphore(name))
            self.val[name] = 0
        self.arena_base = 0
        self.arena_off = 0
        self._uid = 0

    def uid(self):
        self._uid += 1
        return self._uid

    def freeze_arena(self):
        self.arena_base = self.nc.sbuf_base
        self.arena_off = self.arena_base

    def arena_reset(self):
        self.arena_off = self.arena_base

    def tile(self, name, shape, dtype, align=64):
        sz = int(np.prod(shape[1:])) * mybir.dt.size(dtype)
        off = (self.arena_off + align - 1) // align * align
        t = self.nc.alloc_sbuf_tensor_at(f"A{self.uid()}_{name}", shape, dtype, offset=off)
        self.arena_off = off + sz
        assert self.arena_off <= SBUF_TOP, f"SBUF overflow: {self.arena_off} ({name})"
        return t


def barrier(b, blk):
    target = b.val["bar"] + 5
    for reg in (blk.sync, blk.vector, blk.tensor, blk.scalar, blk.gpsimd):
        def fn(eng, target=target):
            eng.nop().then_inc(b.sem["bar"], 1)
            eng.wait_ge(b.sem["bar"], target)
        reg(fn)
    b.val["bar"] = target


def _rope_tables(S_, dims, theta):
    inv = (1.0 / (np.float32(theta) ** (np.arange(0, dims, 2, dtype=np.float32) / np.float32(dims)))).astype(np.float32)
    t = np.arange(S_, dtype=np.float32)
    f = (t[:, None] * inv[None, :]).astype(np.float32)
    emb = np.concatenate([f, f], axis=-1)
    return np.cos(emb), np.sin(emb)


def _rot_matrix(dims):
    R = np.zeros((dims, dims), np.float32)
    half = dims // 2
    for i in range(half):
        R[i, half + i] = -1.0
        R[half + i, i] = 1.0
    return R


def _perm_rho():
    rho = np.arange(I1)
    b0 = rho // 128
    t = (rho % 128) // 64
    a = rho % 64
    return a * 128 + b0 + 64 * t


def _had(n):
    H = np.empty((n, n), np.float32)
    for i in range(n):
        for j in range(n):
            H[i, j] = 1.0 if bin(i & j).count("1") % 2 == 0 else -1.0
    return H


def _host_layout(inputs):
    perm = _perm_rho()
    x = np.asarray(inputs["L0_attn_h"])
    w1T = np.ascontiguousarray(np.asarray(inputs["L1_ffn_W1"])[perm, :].T)
    w3T = np.ascontiguousarray(np.asarray(inputs["L1_ffn_W3"])[perm, :].T)
    w2T = np.ascontiguousarray(np.asarray(inputs["L1_ffn_W2"]).T)
    wqT = np.ascontiguousarray(np.asarray(inputs["L1_attn_Wq"]).T)
    wkT = np.ascontiguousarray(np.asarray(inputs["L1_attn_Wk"]).T)
    wvT = np.ascontiguousarray(np.asarray(inputs["L1_attn_Wv"]).T)
    woT = np.ascontiguousarray(np.asarray(inputs["L1_attn_Wo"]).T)

    cos0, sin0 = _rope_tables(S, QP, 10000.0)
    cos1, sin1 = _rope_tables(S, 128, 1000000.0)

    masks = np.zeros((4, 128, 512), np.float32)
    for mi, delta in enumerate((0, 128, 256, 384)):
        kp = np.arange(128)[:, None]
        qf = np.arange(512)[None, :]
        masks[mi] = np.where(kp + delta <= qf, 0.0, NEG)

    qcols = []
    for h in range(16):
        qcols.extend(range(h * 192, h * 192 + 128))
    for p_ in range(8):
        qcols.extend(range((2 * p_) * 192 + 128, (2 * p_) * 192 + 192))
        qcols.extend(range((2 * p_ + 1) * 192 + 128, (2 * p_ + 1) * 192 + 192))
    qcols = np.asarray(qcols)

    ones_bf = np.ones((1, 128), BF16NP)
    ones_col = np.ones((128, 1), BF16NP)
    r64blk = np.zeros((128, 128), np.float32)
    r64blk[:64, :64] = _rot_matrix(64)
    r64blk[64:, 64:] = _rot_matrix(64)
    R64T = np.ascontiguousarray(r64blk.T).astype(BF16NP)
    R128T = np.ascontiguousarray(_rot_matrix(128).T).astype(BF16NP)
    H128_bf = _had(128).astype(BF16NP)
    H64 = _had(64)
    H64blk = np.zeros((128, 128), np.float32)
    H64blk[:64, :64] = H64
    H64blk[64:, 64:] = H64
    H64_bf = H64blk.astype(BF16NP)

    def duped(tab):
        return np.ascontiguousarray(np.concatenate([tab.T, tab.T], axis=0))

    in_maps = []
    for c in range(NC):
        b_, h_ = c // 2, c % 2
        sl = slice(h_ * T, h_ * T + T)
        m = {
            "x0": np.ascontiguousarray(x[b_, sl, :].T),
            "gate_s": np.ascontiguousarray(np.asarray(inputs["L0_ffn_gate_w"])[c * 256:(c + 1) * 256]),
            "up_s": np.ascontiguousarray(np.asarray(inputs["L0_ffn_up_w"])[c * 256:(c + 1) * 256]),
            "down_s": np.ascontiguousarray(np.asarray(inputs["L0_ffn_down_w"])[c * 768:(c + 1) * 768]),
            "gsc": np.ascontiguousarray(np.asarray(inputs["L0_ffn_gate_scale"]).reshape(48, 128).T),
            "usc": np.ascontiguousarray(np.asarray(inputs["L0_ffn_up_scale"]).reshape(48, 128).T),
            "dsc": np.ascontiguousarray(np.asarray(inputs["L0_ffn_down_scale"]).reshape(16, 128).T),
            "wqa_s": np.ascontiguousarray(np.asarray(inputs["L0_attn_w_q_a"])[c * 256:(c + 1) * 256]),
            "wqb_s": np.ascontiguousarray(np.asarray(inputs["L0_attn_w_q_b"])[c * 192:(c + 1) * 192][:, qcols]),
            "gqa_s": np.ascontiguousarray(np.asarray(inputs["L0_attn_g_q_a"])[c * 192:(c + 1) * 192]).reshape(-1, 1),
            "wkva_s": np.ascontiguousarray(np.asarray(inputs["L0_attn_w_kv_a"])[c * 256:(c + 1) * 256]),
            "wkvb_s": np.ascontiguousarray(np.asarray(inputs["L0_attn_w_kv_b"])[c * 64:(c + 1) * 64]),
            "gkva_s": np.ascontiguousarray(np.asarray(inputs["L0_attn_g_kv_a"])[c * 64:(c + 1) * 64]).reshape(-1, 1),
            "wo_s": np.ascontiguousarray(np.asarray(inputs["L0_attn_w_o"])[c * 256:(c + 1) * 256]),
            "wqT_s": np.ascontiguousarray(wqT[c * 256:(c + 1) * 256]),
            "wkT_s": np.ascontiguousarray(wkT[c * 256:(c + 1) * 256]),
            "wvT_s": np.ascontiguousarray(wvT[c * 256:(c + 1) * 256]),
            "woT_s": np.ascontiguousarray(woT[c * 256:(c + 1) * 256]),
            "w1T_s": np.ascontiguousarray(w1T[c * 256:(c + 1) * 256]),
            "w3T_s": np.ascontiguousarray(w3T[c * 256:(c + 1) * 256]),
            "w2T_s": np.ascontiguousarray(w2T[c * 1024:(c + 1) * 1024]),
            "bq": np.ascontiguousarray(np.asarray(inputs["L1_attn_bq"]).reshape(16, 128).T),
            "bk": np.ascontiguousarray(np.asarray(inputs["L1_attn_bk"]).reshape(16, 128).T),
            "bv_row": np.asarray(inputs["L1_attn_bv"]).reshape(1, 2048).astype(BF16NP),
            "ones_bf": ones_bf,
            "ones_col": ones_col,
            "R64T": R64T, "R128T": R128T, "H128c": H128_bf, "H64c": H64_bf,
            "masks8": (np.concatenate([masks, np.zeros_like(masks)], 0) if h_ == 0
                       else np.concatenate([np.zeros_like(masks), masks], 0)),
            "bmd": np.full((128, 1), NEG if h_ == 0 else EXP_BIAS, np.float32),
            "cos0d": duped(cos0[sl]).astype(np.float32),
            "sin0d": duped(sin0[sl]).astype(np.float32),
            "cos1": np.ascontiguousarray(cos1[sl].T),
            "sin1": np.ascontiguousarray(sin1[sl].T),
        }
        in_maps.append(m)
    return in_maps


def emit_consts(nc, b, blk, io, P):
    jobs = [("ones", "ones_bf"), ("onesc", "ones_col"), ("gsc", "gsc"), ("usc", "usc"),
            ("dsc", "dsc"), ("bq", "bq"), ("bk", "bk"), ("bv", "bv_row"),
            ("R64T", "R64T"), ("R128T", "R128T"), ("H128c", "H128c"), ("H64c", "H64c"),
            ("bmd", "bmd")]

    def fsync(s):
        for dst, src in jobs:
            s.dma_start(out=P[dst][:], in_=io[src]).then_inc(b.sem["ld"], 16)
        for mi in range(8):
            s.dma_start(out=P["masks8"][:, mi * 512:(mi + 1) * 512], in_=io["masks8"][mi]).then_inc(b.sem["ld"], 16)
        b.val["ld"] += 16 * (len(jobs) + 8)
    blk.sync(fsync)
    ldv = b.val["ld"]

    def fvec(v):
        v.wait_ge(b.sem["ld"], ldv)
        v.tensor_scalar(out=P["dsch"][:], in0=P["dsc"][:], scalar1=0.5, scalar2=None, op0=ALU.mult)
        v.memset(P["ballow"][:], EXP_BIAS)
        v.memset(P["bneg"][:], NEG).then_inc(b.sem["dv"], 1)
        b.val["dv"] += 1
    blk.vector(fvec)


def emit_prep(nc, b, blk, io, P, pb):
    b.arena_reset()
    CW = 3072
    qa32 = b.tile("qa32", [128, CW], F32)
    qa8 = b.tile("qa8", [128, CW], F8)
    qab = b.tile("qab", [128, CW], BF16)
    gf32 = b.tile("gf32", [128, 2048], F32)
    gfb = b.tile("gfb", [128, 2048], BF16)
    gv = b.tile("gv", [128, 1], F32)

    qjobs = []
    for src, dst, rows, cols in (("gate_s", "gate_q_loc", 256, I0),
                                 ("up_s", "up_q_loc", 256, I0),
                                 ("down_s", "down_q_loc", 768, D)):
        for r0 in range(0, rows, 128):
            for c0 in range(0, cols, CW):
                qjobs.append((src, dst, r0, c0, min(CW, cols - c0)))
    nq = len(qjobs)
    sA0, sB0, dv0 = b.val["sA"], b.val["sB"], b.val["dv"]

    def fsync(s):
        for k, (src, dst, r0, c0, cols) in enumerate(qjobs):
            if k >= 1:
                s.wait_ge(b.sem["dv"], dv0 + 2 * (k - 1) + 1)
            s.dma_start(out=qa32[:, 0:cols], in_=io[src][r0:r0 + 128, c0:c0 + cols]).then_inc(b.sem["sA"], 16)
        b.val["sA"] += 16 * nq

    def fvec(v):
        for k, (src, dst, r0, c0, cols) in enumerate(qjobs):
            v.wait_ge(b.sem["sA"], sA0 + 16 * (k + 1))
            if k >= 1:
                v.wait_ge(b.sem["sB"], sB0 + 16 * k)
            v.tensor_copy(out=qa8[:, 0:cols], in_=qa32[:, 0:cols]).then_inc(b.sem["dv"], 1)
            v.tensor_copy(out=qab[:, 0:cols], in_=qa8[:, 0:cols]).then_inc(b.sem["dv"], 1)
        b.val["dv"] += 2 * nq

    def fgp(g):
        for k, (src, dst, r0, c0, cols) in enumerate(qjobs):
            g.wait_ge(b.sem["dv"], dv0 + 2 * (k + 1))
            g.dma_start(out=io[dst][r0:r0 + 128, c0:c0 + cols], in_=qab[:, 0:cols]).then_inc(b.sem["sB"], 16)
        b.val["sB"] += 16 * nq
    blk.sync(fsync); blk.vector(fvec); blk.gpsimd(fgp)

    fo_jobs = []
    for src, gsrc, dst, rows, cols in (("wqb_s", "gqa_s", "wqb_loc", 192, 3072),
                                       ("wkvb_s", "gkva_s", "wkvb_loc", 64, 4096)):
        for r0 in range(0, rows, 128):
            for c0 in range(0, cols, 2048):
                fo_jobs.append((src, gsrc, dst, r0, min(128, rows - r0), c0, min(2048, cols - c0)))
    sA1, sB1, dv1 = b.val["sA"], b.val["sB"], b.val["dv"]

    def fsync2(s):
        for k, (src, gsrc, dst, r0, rows, c0, cols) in enumerate(fo_jobs):
            if k >= 1:
                s.wait_ge(b.sem["dv"], dv1 + k)
            s.dma_start(out=gf32[0:rows, 0:cols], in_=io[src][r0:r0 + rows, c0:c0 + cols]).then_inc(b.sem["sA"], 16)
            s.dma_start(out=gv[0:rows, :], in_=io[gsrc][r0:r0 + rows, :]).then_inc(b.sem["sA"], 16)
        b.val["sA"] += 32 * len(fo_jobs)

    def fvec2(v):
        for k, (src, gsrc, dst, r0, rows, c0, cols) in enumerate(fo_jobs):
            v.wait_ge(b.sem["sA"], sA1 + 32 * (k + 1))
            if k >= 1:
                v.wait_ge(b.sem["sB"], sB1 + 16 * k)
            v.tensor_scalar(out=gfb[0:rows, 0:cols], in0=gf32[0:rows, 0:cols],
                            scalar1=gv[0:rows, :], scalar2=None, op0=ALU.mult).then_inc(b.sem["dv"], 1)
        b.val["dv"] += len(fo_jobs)

    def fgp2(g):
        for k, (src, gsrc, dst, r0, rows, c0, cols) in enumerate(fo_jobs):
            g.wait_ge(b.sem["dv"], dv1 + k + 1)
            g.dma_start(out=io[dst][r0:r0 + rows, c0:c0 + cols], in_=gfb[0:rows, 0:cols]).then_inc(b.sem["sB"], 16)
        b.val["sB"] += 16 * len(fo_jobs)
    blk.sync(fsync2); blk.vector(fvec2); blk.gpsimd(fgp2)

    cast_jobs = [("wqa_s", "wqa_loc"), ("wkva_s", "wkva_loc"), ("wo_s", "wo_loc"),
                 ("wqT_s", "wqT_loc"), ("wkT_s", "wkT_loc"), ("wvT_s", "wvT_loc"),
                 ("woT_s", "woT_loc"), ("w1T_s", "w1T_loc"), ("w3T_s", "w3T_loc")]

    def fgp3(g):
        for a_, bb_ in cast_jobs:
            g.dma_start(out=io[bb_], in_=io[a_]).then_inc(b.sem["sB"], 16)
        b.val["sB"] += 16 * len(cast_jobs)
    blk.gpsimd(fgp3)

    w2t32 = b.tile("w2t32", [128, D], F32)
    w2tb = b.tile("w2tb", [128, 2 * D], BF16)
    y2sb = b.tile("y2sb", [128, 2 * D], BF16)
    sA2, sB2, tm0, dv2 = b.val["sA"], b.val["sB"], b.val["tm"], b.val["dv"]

    def fsync3(s):
        for ch in range(8):
            if ch >= 1:
                s.wait_ge(b.sem["dv"], dv2 + 5 * (ch - 1) + 1)
            s.dma_start(out=w2t32[:, :], in_=io["w2T_s"][ch * 128:(ch + 1) * 128, :]).then_inc(b.sem["sA"], 16)
        b.val["sA"] += 16 * 8

    def fvec3(v):
        for ch in range(8):
            half = (ch % 2) * D
            v.wait_ge(b.sem["sA"], sA2 + 16 * (ch + 1))
            if ch >= 2:
                v.wait_ge(b.sem["tm"], tm0 + (ch - 1))
            v.tensor_copy(out=w2tb[:, half:half + D], in_=w2t32[:, :]).then_inc(b.sem["dv"], 1)
            v.wait_ge(b.sem["tm"], tm0 + (ch + 1))
            if ch >= 2:
                v.wait_ge(b.sem["sB"], sB2 + 16 * (ch - 1))
            for j in range(4):
                v.tensor_copy(out=y2sb[:, half + j * 512: half + (j + 1) * 512],
                              in_=pb[4 + j][0:128, :]).then_inc(b.sem["dv"], 1)
        b.val["dv"] += 5 * 8

    def ftm(t):
        for ch in range(8):
            half = (ch % 2) * D
            t.wait_ge(b.sem["dv"], dv2 + 5 * ch + 1)
            if ch >= 1:
                t.wait_ge(b.sem["dv"], dv2 + 5 * ch)
            t.ldweights(P["H128c"][:])
            last = None
            for j in range(4):
                last = t.matmul(pb[4 + j][0:128, :], P["H128c"][:], w2tb[:, half + j * 512: half + (j + 1) * 512],
                                start=True, stop=True)
            last.then_inc(b.sem["tm"], 1)
        b.val["tm"] += 8

    def fgp4(g):
        for ch in range(8):
            half = (ch % 2) * D
            g.wait_ge(b.sem["dv"], dv2 + 5 * (ch + 1))
            g.dma_start(out=io["y2_loc"][ch * 128:(ch + 1) * 128, :], in_=y2sb[:, half:half + D]).then_inc(b.sem["sB"], 16)
        b.val["sB"] += 16 * 8
    blk.sync(fsync3); blk.vector(fvec3); blk.tensor(ftm); blk.gpsimd(fgp4)

    ag1 = [("gate_q_loc", "gate_q"), ("up_q_loc", "up_q"), ("down_q_loc", "down_q"),
           ("wqb_loc", "wqb_b"), ("wkvb_loc", "wkvb_b"),
           ("wqa_loc", "wqa_b"), ("wkva_loc", "wkva_b"), ("wo_loc", "wo_b"),
           ("wqT_loc", "wqT_b"), ("wkT_loc", "wkT_b"), ("wvT_loc", "wvT_b"),
           ("woT_loc", "woT_b"), ("w1T_loc", "w1T_b"), ("w3T_loc", "w3T_b"),
           ("y2_loc", "y2_b")]
    sB_now = b.val["sB"]

    def fgp5(g):
        g.wait_ge(b.sem["sB"], sB_now)
        for a_, bb_ in ag1:
            g.collective_compute("AllGather", ALU.bypass, replica_groups=[list(range(8))],
                                 ins=[io[a_]], outs=[io[bb_]]).then_inc(b.sem["cc"], 1)
        b.val["cc"] += len(ag1)
    blk.gpsimd(fgp5)

    ysb = b.tile("ysb", [128, 2 * D], BF16)
    w2fsb = b.tile("w2fsb", [128, 2 * D], BF16)
    cc_y2 = b.val["cc"]
    tm1, dv3 = b.val["tm"], b.val["dv"]
    sC0, sD0, sE0, sF0 = b.val["sC"], b.val["sD"], b.val["sE"], b.val["sF"]
    y2r = io["y2_b"].rearrange("(a bb) d -> bb a d", bb=128)

    def fsync4(s):
        s.wait_ge(b.sem["cc"], cc_y2)
        for g0 in range(64):
            sem = "sC" if g0 % 2 == 0 else "sD"
            if g0 >= 2:
                s.wait_ge(b.sem["tm"], tm1 + (g0 - 1))
            s.dma_start(out=ysb[0:64, (g0 % 2) * D:(g0 % 2) * D + D], in_=y2r[g0]).then_inc(b.sem[sem], 16)
            s.dma_start(out=ysb[64:128, (g0 % 2) * D:(g0 % 2) * D + D], in_=y2r[g0 + 64]).then_inc(b.sem[sem], 16)
            b.val[sem] += 32

    def ftm2(t):
        for g0 in range(64):
            slot = g0 % 2
            half = slot * D
            t.wait_ge(b.sem["sC" if slot == 0 else "sD"], (sC0 if slot == 0 else sD0) + 32 * (g0 // 2 + 1))
            if g0 >= 1:
                t.wait_ge(b.sem["dv"], dv3 + 4 * g0)
            t.ldweights(P["H64c"][:])
            last = None
            for j in range(4):
                last = t.matmul(pb[4 + j][0:128, :], P["H64c"][:], ysb[:, half + j * 512: half + (j + 1) * 512],
                                start=True, stop=True)
            last.then_inc(b.sem["tm"], 1)
        b.val["tm"] += 64

    def fvec4(v):
        for g0 in range(64):
            slot = g0 % 2
            half = slot * D
            v.wait_ge(b.sem["tm"], tm1 + (g0 + 1))
            if g0 >= 2:
                v.wait_ge(b.sem["sE" if slot == 0 else "sF"], (sE0 if slot == 0 else sF0) + 16 * (g0 // 2))
            for j in range(4):
                v.tensor_copy(out=w2fsb[:, half + j * 512: half + (j + 1) * 512],
                              in_=pb[4 + j][0:128, :]).then_inc(b.sem["dv"], 1)
        b.val["dv"] += 4 * 64

    def fgp6(g):
        for g0 in range(64):
            slot = g0 % 2
            sem = "sE" if slot == 0 else "sF"
            half = slot * D
            g.wait_ge(b.sem["dv"], dv3 + 4 * (g0 + 1))
            g.dma_start(out=io["w2fT_b"][g0 * 128:(g0 + 1) * 128, :], in_=w2fsb[:, half:half + D]).then_inc(b.sem[sem], 16)
            b.val[sem] += 16
    blk.sync(fsync4); blk.tensor(ftm2); blk.vector(fvec4); blk.gpsimd(fgp6)

    cc_fin = b.val["cc"]
    sE_fin, sF_fin = b.val["sE"], b.val["sF"]

    def fjoin(g):
        g.wait_ge(b.sem["cc"], cc_fin)
        g.wait_ge(b.sem["sE"], sE_fin)
        g.wait_ge(b.sem["sF"], sF_fin)
    blk.gpsimd(fjoin)


def emit_ln(nc, b, blk, io, P, pb, xsrc, xn):
    xt = b.tile("xt", [128, 2 * T], F32)
    xb = b.tile("xb", [128, 2 * T], BF16)
    x2b = b.tile("x2b", [128, 2 * T], BF16)
    stat = b.tile("stat", [1, 3 * T], F32)
    statb = b.tile("statb", [1, 2 * T], BF16)
    mb = b.tile("mb", [128, 2 * T], F32)

    dv0, ac0, tm0 = b.val["dv"], b.val["ac"], b.val["tm"]
    sA0, sB0 = b.val["sA"], b.val["sG"]

    def fsync(s):
        for c in range(DC):
            if c >= 2:
                s.wait_ge(b.sem["ac"], ac0 + (c - 1))
                s.wait_ge(b.sem["dv"], dv0 + (c - 1))
            sem = "sA" if c % 2 == 0 else "sG"
            s.dma_start(out=xt[:, (c % 2) * T:(c % 2 + 1) * T], in_=xsrc[c * 128:(c + 1) * 128, :]).then_inc(b.sem[sem], 16)
            b.val[sem] += 16

    def fvec(v):
        for c in range(DC):
            sem, base = ("sA", sA0) if c % 2 == 0 else ("sG", sB0)
            v.wait_ge(b.sem[sem], base + 16 * (c // 2 + 1))
            if c >= 2:
                v.wait_ge(b.sem["tm"], tm0 + (c - 1))
            v.tensor_copy(out=xb[:, (c % 2) * T:(c % 2 + 1) * T], in_=xt[:, (c % 2) * T:(c % 2 + 1) * T]).then_inc(b.sem["dv"], 1)
        b.val["dv"] += DC

    def fact(sc):
        for c in range(DC):
            sem, base = ("sA", sA0) if c % 2 == 0 else ("sG", sB0)
            sc.wait_ge(b.sem[sem], base + 16 * (c // 2 + 1))
            if c >= 2:
                sc.wait_ge(b.sem["tm"], tm0 + (c - 1))
            sc.activation(out=x2b[:, (c % 2) * T:(c % 2 + 1) * T], in_=xt[:, (c % 2) * T:(c % 2 + 1) * T],
                          func=AF.Square).then_inc(b.sem["ac"], 1)
        b.val["ac"] += DC

    def ftm(t):
        for c in range(DC):
            t.wait_ge(b.sem["dv"], dv0 + c + 1)
            t.wait_ge(b.sem["ac"], ac0 + c + 1)
            t.ldweights(P["onesc"][:])
            last = None
            for j in range(4):
                t.matmul(pb[j][0:1, :], P["onesc"][:],
                         xb[:, (c % 2) * T + j * 512:(c % 2) * T + (j + 1) * 512],
                         start=(c == 0), stop=(c == DC - 1))
                last = t.matmul(pb[4 + j][0:1, :], P["onesc"][:],
                                x2b[:, (c % 2) * T + j * 512:(c % 2) * T + (j + 1) * 512],
                                start=(c == 0), stop=(c == DC - 1))
            last.then_inc(b.sem["tm"], 1)
        b.val["tm"] += DC
    blk.sync(fsync); blk.vector(fvec); blk.scalar(fact); blk.tensor(ftm)

    tmv = b.val["tm"]

    def fvec2(v):
        v.wait_ge(b.sem["tm"], tmv)
        for j in range(4):
            v.tensor_scalar(out=stat[:, j * 512:(j + 1) * 512], in0=pb[j][0:1, :],
                            scalar1=1.0 / D, scalar2=None, op0=ALU.mult)
            v.tensor_scalar(out=stat[:, T + j * 512: T + (j + 1) * 512], in0=pb[4 + j][0:1, :],
                            scalar1=1.0 / D, scalar2=None, op0=ALU.mult)
        v.tensor_tensor(out=stat[:, 2 * T:3 * T], in0=stat[:, 0:T], in1=stat[:, 0:T], op=ALU.mult)
        v.tensor_tensor(out=stat[:, 2 * T:3 * T], in0=stat[:, T:2 * T], in1=stat[:, 2 * T:3 * T], op=ALU.subtract)
        v.tensor_scalar(out=stat[:, 2 * T:3 * T], in0=stat[:, 2 * T:3 * T], scalar1=1e-5, scalar2=None,
                        op0=ALU.add).then_inc(b.sem["dv"], 1)
        b.val["dv"] += 1
    blk.vector(fvec2)
    dvv = b.val["dv"]

    def fact2(sc):
        sc.wait_ge(b.sem["dv"], dvv)
        sc.activation(out=stat[:, T:2 * T], in_=stat[:, 2 * T:3 * T], func=AF.Sqrt).then_inc(b.sem["ac"], 1)
        b.val["ac"] += 1
    blk.scalar(fact2)
    acv = b.val["ac"]

    def fvec3(v):
        v.wait_ge(b.sem["ac"], acv)
        v.reciprocal(out=stat[:, 2 * T:3 * T], in_=stat[:, T:2 * T])
        v.tensor_tensor(out=stat[:, 0:T], in0=stat[:, 0:T], in1=stat[:, 2 * T:3 * T], op=ALU.mult)
        v.tensor_copy(out=statb[:, 0:T], in_=stat[:, 0:T])
        v.tensor_copy(out=statb[:, T:2 * T], in_=stat[:, 2 * T:3 * T]).then_inc(b.sem["dv"], 1)
        b.val["dv"] += 1
    blk.vector(fvec3)
    dvv2 = b.val["dv"]

    def ftm2(t):
        t.wait_ge(b.sem["dv"], dvv2)
        t.ldweights(P["ones"][:])
        last = None
        for j in range(8):
            last = t.matmul(pb[j][0:128, :], P["ones"][:], statb[:, j * 512:(j + 1) * 512],
                            start=True, stop=True)
        last.then_inc(b.sem["tm"], 1)
        b.val["tm"] += 1
    blk.tensor(ftm2)
    tmv2 = b.val["tm"]

    def fvec4(v):
        v.wait_ge(b.sem["tm"], tmv2)
        for j in range(8):
            op = v.tensor_copy(out=mb[:, j * 512:(j + 1) * 512], in_=pb[j][0:128, :])
        op.then_inc(b.sem["dv"], 1)
        b.val["dv"] += 1
    blk.vector(fvec4)
    dvv3 = b.val["dv"]
    sA1, sB1 = b.val["sA"], b.val["sG"]

    def fsync5(s):
        s.wait_ge(b.sem["dv"], dvv3)
        for c in range(DC):
            if c >= 2:
                s.wait_ge(b.sem["dv"], dvv3 + 2 * (c - 1))
            sem = "sA" if c % 2 == 0 else "sG"
            s.dma_start(out=xt[:, (c % 2) * T:(c % 2 + 1) * T], in_=xsrc[c * 128:(c + 1) * 128, :]).then_inc(b.sem[sem], 16)
            b.val[sem] += 16

    def fvec5(v):
        for c in range(DC):
            sem, base = ("sA", sA1) if c % 2 == 0 else ("sG", sB1)
            v.wait_ge(b.sem[sem], base + 16 * (c // 2 + 1))
            v.tensor_tensor(out=xt[:, (c % 2) * T:(c % 2 + 1) * T], in0=xt[:, (c % 2) * T:(c % 2 + 1) * T],
                            in1=mb[:, T:2 * T], op=ALU.mult).then_inc(b.sem["dv"], 1)
            v.tensor_tensor(out=xn[:, c * T:(c + 1) * T], in0=xt[:, (c % 2) * T:(c % 2 + 1) * T],
                            in1=mb[:, 0:T], op=ALU.subtract).then_inc(b.sem["dv"], 1)
        b.val["dv"] += 2 * DC
    blk.sync(fsync5); blk.vector(fvec5)


def emit_ffn(nc, b, blk, io, P, pb, xn, layer, xsrc, xdst):
    b.arena_reset()
    FI = I0 if layer == 0 else I1
    FC = FI // 128
    wg_h = io["gate_q"] if layer == 0 else io["w1T_b"]
    wu_h = io["up_q"] if layer == 0 else io["w3T_b"]
    wd_h = io["down_q"] if layer == 0 else io["w2fT_b"]
    wgt = b.tile("wgt", [128, 2 * D], BF16)
    wut = b.tile("wut", [128, 2 * D], BF16)
    silu_t = b.tile("silu", [128, 4 * 512], BF16)
    hidt = b.tile("hidt", [128, 2 * T], BF16)

    wg_r = wg_h.rearrange("(c p) n -> p c n", p=128)
    wu_r = wu_h.rearrange("(c p) n -> p c n", p=128)

    tm0, ac0, dv0 = b.val["tm"], b.val["ac"], b.val["dv"]
    sC0, sD0, sE0, sF0 = b.val["sC"], b.val["sD"], b.val["sE"], b.val["sF"]

    def fsyncA(s):
        for f in range(FC):
            if f >= 2:
                s.wait_ge(b.sem["tm"], tm0 + 2 * (f - 1))
            half = (f % 2) * D
            sem = "sC" if f % 2 == 0 else "sD"
            s.dma_start(out=wgt[:, half:half + D].rearrange("p (c n) -> p c n", n=128),
                        in_=wg_r[:, :, f * 128:(f + 1) * 128]).then_inc(b.sem[sem], 16)
            s.dma_start(out=wut[:, half:half + D].rearrange("p (c n) -> p c n", n=128),
                        in_=wu_r[:, :, f * 128:(f + 1) * 128]).then_inc(b.sem[sem], 16)
            b.val[sem] += 32

    def ftmA(t):
        for f in range(FC):
            half = (f % 2) * D
            t.wait_ge(b.sem["sC" if f % 2 == 0 else "sD"], (sC0 if f % 2 == 0 else sD0) + 32 * (f // 2 + 1))
            for tset in range(2):
                if f >= 1:
                    t.wait_ge(b.sem["dv"], dv0 + 4 * (f - 1) + 2 * tset + 2)
                base_b = 4 * tset
                last = None
                for c in range(DC):
                    t.ldweights(wgt[:, half + c * 128: half + (c + 1) * 128])
                    for u in range(2):
                        tt = 2 * tset + u
                        t.matmul(pb[base_b + u][0:128, :], wgt[:, half + c * 128: half + (c + 1) * 128],
                                 xn[:, c * T + tt * 512: c * T + (tt + 1) * 512],
                                 start=(c == 0), stop=(c == DC - 1))
                    t.ldweights(wut[:, half + c * 128: half + (c + 1) * 128])
                    for u in range(2):
                        tt = 2 * tset + u
                        last = t.matmul(pb[base_b + 2 + u][0:128, :], wut[:, half + c * 128: half + (c + 1) * 128],
                                        xn[:, c * T + tt * 512: c * T + (tt + 1) * 512],
                                        start=(c == 0), stop=(c == DC - 1))
                last.then_inc(b.sem["tm"], 1)
        b.val["tm"] += 2 * FC

    def factA(sc):
        for f in range(FC):
            for tset in range(2):
                sc.wait_ge(b.sem["tm"], tm0 + 2 * f + tset + 1)
                for u in range(2):
                    tt = 2 * tset + u
                    if f >= 1:
                        sc.wait_ge(b.sem["dv"], dv0 + 4 * (f - 1) + tt + 1)
                    kw = dict(scale=P["gsc"][:, f:f + 1]) if layer == 0 else {}
                    sc.activation(out=silu_t[:, tt * 512:(tt + 1) * 512],
                                  in_=pb[4 * tset + u][0:128, :], func=AF.Silu, **kw).then_inc(b.sem["ac"], 1)
        b.val["ac"] += 4 * FC

    def fvecA(v):
        for f in range(FC):
            fh = (f % 2) * T
            sem, base = ("sE", sE0) if f % 2 == 0 else ("sF", sF0)
            for tset in range(2):
                base_b = 4 * tset
                for u in range(2):
                    tt = 2 * tset + u
                    v.wait_ge(b.sem["ac"], ac0 + 4 * f + 2 * tset + u + 1)
                    if f >= 2 and tset == 0 and u == 0:
                        v.wait_ge(b.sem[sem], base + 16 * (f // 2))
                    if layer == 0:
                        v.tensor_scalar(out=pb[base_b + 2 + u][0:128, :], in0=pb[base_b + 2 + u][0:128, :],
                                        scalar1=P["usc"][:, f:f + 1], scalar2=None, op0=ALU.mult)
                    v.tensor_tensor(out=hidt[:, fh + tt * 512:fh + (tt + 1) * 512], in0=pb[base_b + 2 + u][0:128, :],
                                    in1=silu_t[:, tt * 512:(tt + 1) * 512], op=ALU.mult).then_inc(b.sem["dv"], 1)
        b.val["dv"] += 4 * FC

    def fgpA(g):
        for f in range(FC):
            fh = (f % 2) * T
            sem = "sE" if f % 2 == 0 else "sF"
            g.wait_ge(b.sem["dv"], dv0 + 4 * (f + 1))
            g.dma_start(out=io["hid"][f * 128:(f + 1) * 128, :], in_=hidt[:, fh:fh + T]).then_inc(b.sem[sem], 16)
            b.val[sem] += 16
        g.wait_ge(b.sem["sE"], b.val["sE"])
        g.wait_ge(b.sem["sF"], b.val["sF"])
    blk.sync(fsyncA); blk.tensor(ftmA); blk.scalar(factA); blk.vector(fvecA); blk.gpsimd(fgpA)

    barrier(b, blk)

    b.arena_reset()
    hload = b.tile("hload", [128, 2 * 512], BF16)
    wdt = b.tile("wdt", [128, 2 * 1024], BF16)
    xrt = b.tile("xrt", [128, 8 * 512], F32)
    xnew = b.tile("xnew", [128, 8 * 512], F32)
    tm1, dv1 = b.val["tm"], b.val["dv"]
    sC1, sD1, sG0, sH0 = b.val["sC"], b.val["sD"], b.val["sG"], b.val["sH"]

    def fsyncB(s):
        gi = 0
        for tq in range(4):
            for dg in range(2):
                if gi >= 1:
                    s.wait_ge(b.sem["dv"], dv1 + 8 * gi)
                s.dma_start(out=xrt[:].rearrange("p (c n) -> p c n", n=512),
                            in_=xsrc.rearrange("(c p) t -> p c t", p=128)[:, dg * 8:(dg + 1) * 8, tq * 512:(tq + 1) * 512]).then_inc(b.sem["sG"], 16)
                b.val["sG"] += 16
                for f in range(FC):
                    k = gi * FC + f
                    if k >= 2:
                        s.wait_ge(b.sem["tm"], tm1 + (k - 1))
                    sem = "sC" if k % 2 == 0 else "sD"
                    s.dma_start(out=hload[:, (k % 2) * 512:(k % 2 + 1) * 512],
                                in_=io["hid"][f * 128:(f + 1) * 128, tq * 512:(tq + 1) * 512]).then_inc(b.sem[sem], 16)
                    s.dma_start(out=wdt[:, (k % 2) * 1024:(k % 2 + 1) * 1024],
                                in_=wd_h[f * 128:(f + 1) * 128, dg * 1024:(dg + 1) * 1024]).then_inc(b.sem[sem], 16)
                    b.val[sem] += 32
                gi += 1

    def ftmB(t):
        gi = 0
        for tq in range(4):
            for dg in range(2):
                for f in range(FC):
                    k = gi * FC + f
                    t.wait_ge(b.sem["sC" if k % 2 == 0 else "sD"], (sC1 if k % 2 == 0 else sD1) + 32 * (k // 2 + 1))
                    if f == 0 and gi >= 1:
                        t.wait_ge(b.sem["dv"], dv1 + 8 * gi)
                    last = None
                    for dch in range(8):
                        t.ldweights(wdt[:, (k % 2) * 1024 + dch * 128:(k % 2) * 1024 + (dch + 1) * 128])
                        last = t.matmul(pb[dch][0:128, :],
                                        wdt[:, (k % 2) * 1024 + dch * 128:(k % 2) * 1024 + (dch + 1) * 128],
                                        hload[:, (k % 2) * 512:(k % 2 + 1) * 512],
                                        start=(f == 0), stop=(f == FC - 1))
                    last.then_inc(b.sem["tm"], 1)
                gi += 1
        b.val["tm"] += FC * 8

    def fvecB(v):
        gi = 0
        for tq in range(4):
            for dg in range(2):
                v.wait_ge(b.sem["tm"], tm1 + (gi + 1) * FC)
                v.wait_ge(b.sem["sG"], sG0 + 16 * (gi + 1))
                if gi >= 1:
                    v.wait_ge(b.sem["sH"], sH0 + 16 * gi)
                for dch in range(8):
                    if layer == 0:
                        v.scalar_tensor_tensor(out=xnew[:, dch * 512:(dch + 1) * 512], in0=pb[dch][0:128, :],
                                               scalar=P["dsch"][:, dg * 8 + dch: dg * 8 + dch + 1],
                                               in1=xrt[:, dch * 512:(dch + 1) * 512],
                                               op0=ALU.mult, op1=ALU.add).then_inc(b.sem["dv"], 1)
                    else:
                        v.scalar_tensor_tensor(out=xnew[:, dch * 512:(dch + 1) * 512], in0=pb[dch][0:128, :],
                                               scalar=0.5 / math.sqrt(I1),
                                               in1=xrt[:, dch * 512:(dch + 1) * 512],
                                               op0=ALU.mult, op1=ALU.add).then_inc(b.sem["dv"], 1)
                gi += 1
        b.val["dv"] += 8 * 8

    def fgpB(g):
        gi = 0
        for tq in range(4):
            for dg in range(2):
                g.wait_ge(b.sem["dv"], dv1 + 8 * (gi + 1))
                g.dma_start(out=xdst.rearrange("(c p) t -> p c t", p=128)[:, dg * 8:(dg + 1) * 8, tq * 512:(tq + 1) * 512],
                            in_=xnew[:].rearrange("p (c n) -> p c n", n=512)).then_inc(b.sem["sH"], 16)
                b.val["sH"] += 16
                gi += 1
        g.wait_ge(b.sem["sH"], b.val["sH"])
    blk.sync(fsyncB); blk.tensor(ftmB); blk.vector(fvecB); blk.gpsimd(fgpB)


ATTN_PARTS = "full"
OPROJ_IN1 = "xr"
SC0 = 1.0 / math.sqrt(192.0)
SC1 = 1.0 / float(np.float32(np.sqrt(np.float32(128.0))))


def _cls_l0(qt, kc):
    region = kc // 16
    kcr = kc % 16
    k0, k1 = kcr * 128, kcr * 128 + 127
    q0, q1 = qt * 512, qt * 512 + 511
    if k1 <= q0:
        rel = ("allow", None)
    elif k0 > q1:
        rel = ("deny", None)
    else:
        rel = ("diag", (k0 - q0) // 128)
    if region == 0:
        if rel[0] == "allow":
            return ("allow", None)
        if rel[0] == "deny":
            return ("bmd", None)
        return ("allow", rel[1])
    else:
        if rel[0] == "allow":
            return ("bmd", None)
        if rel[0] == "deny":
            return ("neg", None)
        return ("bmd", 4 + rel[1])


def _off_scores(kc):
    return kc + 1 + max(0, kc - 3)


def _off_pv(kcp):
    return (2 * kcp + 5) if kcp <= 28 else (33 + kcp)


def _attn0_kv(nc, b, blk, io, P, pb, xn):
    b.arena_reset()
    wkva = b.tile("wkva", [128, 2 * D], BF16)
    lat32 = b.tile("lat32", [128, 4 * T], F32)
    kpe32 = b.tile("kpe32", [128, T], F32)
    sqb = b.tile("sqb", [128, 2 * 512], BF16)
    rs = b.tile("rs", [1, 2 * T], F32)
    rsb = b.tile("rsb", [1, T], BF16)
    rb32 = b.tile("rb32", [128, T], F32)
    latb = b.tile("latb", [128, 4 * T], BF16)
    cs0 = b.tile("cs0", [128, T], F32)
    sn0 = b.tile("sn0", [128, T], F32)
    kpeb = b.tile("kpeb", [128, T], BF16)
    ropt = rb32
    krotb = b.tile("krotb", [128, T], BF16)

    sA0, sC9, sD9 = b.val["sA"], b.val["sC"], b.val["sD"]
    tm0, ac0, dv0 = b.val["tm"], b.val["ac"], b.val["dv"]
    widths = [128, 128, 128, 128, 64]
    OFFP = [0, 5, 10, 15, 20]
    wkva_r = io["wkva_b"].rearrange("(c p) n -> p c n", p=128)

    def fsync(s):
        s.dma_start(out=cs0[:], in_=io["cos0d"]).then_inc(b.sem["sA"], 16)
        s.dma_start(out=sn0[:], in_=io["sin0d"]).then_inc(b.sem["sA"], 16)
        b.val["sA"] += 32
        for fc in range(5):
            w = widths[fc]
            if fc >= 2:
                s.wait_ge(b.sem["tm"], tm0 + OFFP[fc - 1] + 1)
            sem = "sC" if fc % 2 == 0 else "sD"
            s.dma_start(out=wkva[:, (fc % 2) * D:(fc % 2) * D + 16 * w].rearrange("p (c n) -> p c n", n=w),
                        in_=wkva_r[:, :, fc * 128: fc * 128 + w]).then_inc(b.sem[sem], 16)
            b.val[sem] += 16
    blk.sync(fsync)

    def ftm(t):
        t.wait_ge(b.sem["sA"], sA0 + 32)
        for fc in range(5):
            w = widths[fc]
            t.wait_ge(b.sem["sC" if fc % 2 == 0 else "sD"], (sC9 if fc % 2 == 0 else sD9) + 16 * (fc // 2 + 1))
            if fc >= 1:
                t.wait_ge(b.sem["dv"], dv0 + 4 * fc)
                t.wait_ge(b.sem["ac"], ac0 + 4 * min(fc, 4))
            last = None
            for c in range(DC):
                t.ldweights(wkva[:, (fc % 2) * D + c * w: (fc % 2) * D + (c + 1) * w])
                for tt in range(4):
                    last = t.matmul(pb[tt][0:w, :], wkva[:, (fc % 2) * D + c * w: (fc % 2) * D + (c + 1) * w],
                                    xn[:, c * T + tt * 512: c * T + (tt + 1) * 512],
                                    start=(c == 0), stop=(c == DC - 1))
            last.then_inc(b.sem["tm"], 1)
            if fc < 4:
                for tt in range(4):
                    t.wait_ge(b.sem["ac"], ac0 + 4 * fc + tt + 1)
                    t.ldweights(P["onesc"][:])
                    t.matmul(pb[4 + tt][0:1, :], P["onesc"][:], sqb[:, (tt % 2) * 512:(tt % 2 + 1) * 512],
                             start=(fc == 0), stop=(fc == 3)).then_inc(b.sem["tm"], 1)
        b.val["tm"] += 21
    blk.tensor(ftm)

    def fact(sc):
        for fc in range(4):
            for tt in range(4):
                sc.wait_ge(b.sem["tm"], tm0 + OFFP[fc] + 1)
                if fc >= 1 or tt >= 2:
                    pfc, ptt = (fc, tt - 2) if tt >= 2 else (fc - 1, tt + 2)
                    sc.wait_ge(b.sem["tm"], tm0 + OFFP[pfc] + 1 + ptt + 1)
                sc.activation(out=sqb[:, (tt % 2) * 512:(tt % 2 + 1) * 512], in_=pb[tt][0:128, :],
                              func=AF.Square).then_inc(b.sem["ac"], 1)
        b.val["ac"] += 16
    blk.scalar(fact)

    def fvec(v):
        for fc in range(5):
            v.wait_ge(b.sem["tm"], tm0 + OFFP[fc] + 1)
            for tt in range(4):
                if fc < 4:
                    v.wait_ge(b.sem["ac"], ac0 + 4 * fc + tt + 1)
                    v.tensor_copy(out=lat32[:, fc * T + tt * 512: fc * T + (tt + 1) * 512],
                                  in_=pb[tt][0:128, :]).then_inc(b.sem["dv"], 1)
                else:
                    v.tensor_copy(out=kpe32[0:64, tt * 512:(tt + 1) * 512],
                                  in_=pb[tt][0:64, :]).then_inc(b.sem["dv"], 1)
        b.val["dv"] += 20
    blk.vector(fvec)

    tm1, dv1, ac1 = b.val["tm"], b.val["dv"], b.val["ac"]

    def fvec2(v):
        v.wait_ge(b.sem["tm"], tm1)
        for tt in range(4):
            op = v.tensor_scalar(out=rs[:, tt * 512:(tt + 1) * 512], in0=pb[4 + tt][0:1, :],
                                 scalar1=1.0 / KVR, scalar2=1e-6, op0=ALU.mult, op1=ALU.add)
        op.then_inc(b.sem["dv"], 1)
        b.val["dv"] += 1
    blk.vector(fvec2)
    dvv = b.val["dv"]

    def fact2(sc):
        sc.wait_ge(b.sem["dv"], dvv)
        sc.activation(out=rs[:, T:2 * T], in_=rs[:, 0:T], func=AF.Sqrt).then_inc(b.sem["ac"], 1)
        b.val["ac"] += 1
    blk.scalar(fact2)
    acv = b.val["ac"]

    def fvec3(v):
        v.wait_ge(b.sem["ac"], acv)
        v.reciprocal(out=rs[:, 0:T], in_=rs[:, T:2 * T])
        v.tensor_copy(out=rsb[:], in_=rs[:, 0:T])
        v.tensor_copy(out=kpeb[0:64, :], in_=kpe32[0:64, :]).then_inc(b.sem["dv"], 1)
        b.val["dv"] += 1
    blk.vector(fvec3)
    dvv2 = b.val["dv"]

    def ftm2(t):
        t.wait_ge(b.sem["dv"], dvv2)
        t.ldweights(P["ones"][:])
        last = None
        for tt in range(4):
            last = t.matmul(pb[4 + tt][0:128, :], P["ones"][:], rsb[:, tt * 512:(tt + 1) * 512],
                            start=True, stop=True)
        last.then_inc(b.sem["tm"], 1)
        t.ldweights(P["R64T"][0:64, 0:64])
        last = None
        for tt in range(4):
            last = t.matmul(pb[tt][0:64, :], P["R64T"][0:64, 0:64], kpeb[0:64, tt * 512:(tt + 1) * 512],
                            start=True, stop=True)
        last.then_inc(b.sem["tm"], 1)
        b.val["tm"] += 2
    blk.tensor(ftm2)
    tm2 = b.val["tm"]

    def fvec4(v):
        v.wait_ge(b.sem["tm"], tm2 - 1)
        for c in range(4):
            v.tensor_copy(out=rb32[:, c * 512:(c + 1) * 512], in_=pb[4 + c][0:128, :])
        for c in range(4):
            v.tensor_tensor(out=latb[:, c * T:(c + 1) * T], in0=lat32[:, c * T:(c + 1) * T],
                            in1=rb32[:], op=ALU.mult).then_inc(b.sem["dv"], 1)
        v.wait_ge(b.sem["tm"], tm2)
        v.tensor_tensor(out=ropt[0:64, :], in0=kpe32[0:64, :], in1=cs0[0:64, :], op=ALU.mult)
        for tt in range(4):
            v.scalar_tensor_tensor(out=kpe32[0:64, tt * 512:(tt + 1) * 512], in0=pb[tt][0:64, :],
                                   scalar=1.0, in1=sn0[0:64, tt * 512:(tt + 1) * 512],
                                   op0=ALU.mult, op1=ALU.mult)
        v.tensor_tensor(out=krotb[0:64, :], in0=ropt[0:64, :], in1=kpe32[0:64, :], op=ALU.add).then_inc(b.sem["dv"], 1)
        b.val["dv"] += 5
    blk.vector(fvec4)
    dvv3 = b.val["dv"]
    sE0 = b.val["sE"]

    def fgp(g):
        g.wait_ge(b.sem["dv"], dvv3 - 1)
        for c in range(4):
            g.dma_start(out=io["lat_loc"][c * 128:(c + 1) * 128, :], in_=latb[:, c * T:(c + 1) * T]).then_inc(b.sem["sE"], 16)
        g.wait_ge(b.sem["dv"], dvv3)
        g.dma_start(out=io["lat_loc"][512:576, :], in_=krotb[0:64, :]).then_inc(b.sem["sE"], 16)
        g.wait_ge(b.sem["sE"], sE0 + 80)
        b.val["sE"] += 80
        if ATTN_PARTS != "lat":
            g.collective_compute("AllGather", ALU.bypass, replica_groups=[[0, 1], [2, 3], [4, 5], [6, 7]],
                                 ins=[io["lat_loc"]], outs=[io["lat_all"]]).then_inc(b.sem["cc"], 1)
            b.val["cc"] += 1
    blk.gpsimd(fgp)

    barrier(b, blk)
    if ATTN_PARTS in ("lat", "ag"):
        return

    b.arena_reset()
    latf = b.tile("latf", [128, 4 * S], BF16)
    wkvbv = b.tile("wkvbv", [128, 4 * 2048], BF16)
    wkvbn = b.tile("wkvbn", [128, 2 * 512], BF16)
    keb = b.tile("keb", [128, 2 * 512], BF16)
    veb = b.tile("veb", [128, 2 * 2048], BF16)

    cc0 = b.val["cc"]
    sA1, sC1, sD1 = b.val["sA"], b.val["sC"], b.val["sD"]
    tm3, dv3 = b.val["tm"], b.val["dv"]
    sE1, sF1 = b.val["sE"], b.val["sF"]
    wkvb_r = io["wkvb_b"].rearrange("(c p) (h j) -> p c h j", p=128, j=256)

    def fsync2(s):
        s.wait_ge(b.sem["cc"], cc0)
        for c in range(4):
            s.dma_start(out=latf[:, c * S: c * S + T], in_=io["lat_all"][c * 128:(c + 1) * 128, :]).then_inc(b.sem["sA"], 16)
            s.dma_start(out=latf[:, c * S + T: (c + 1) * S], in_=io["lat_all"][576 + c * 128: 576 + (c + 1) * 128, :]).then_inc(b.sem["sA"], 16)
        for c in range(4):
            s.dma_start(out=wkvbv[:, c * 2048:(c + 1) * 2048].rearrange("p (h j) -> p h j", j=128),
                        in_=wkvb_r[:, c, :, 128:256]).then_inc(b.sem["sA"], 16)
        b.val["sA"] += 16 * 12
        for h in range(NH):
            if h >= 2:
                s.wait_ge(b.sem["tm"], tm3 + 8 * (h - 1))
            sem = "sC" if h % 2 == 0 else "sD"
            s.dma_start(out=wkvbn[:, (h % 2) * 512:(h % 2 + 1) * 512].rearrange("p (c n) -> p c n", n=128),
                        in_=wkvb_r[:, :, h, 0:128]).then_inc(b.sem[sem], 16)
            b.val[sem] += 16
    blk.sync(fsync2)

    def ftm3(t):
        t.wait_ge(b.sem["sA"], sA1 + 16 * 12)
        for h in range(NH):
            t.wait_ge(b.sem["sC" if h % 2 == 0 else "sD"], (sC1 if h % 2 == 0 else sD1) + 16 * (h // 2 + 1))
            for kt in range(8):
                if h * 8 + kt >= 2:
                    t.wait_ge(b.sem["dv"], dv3 + (h * 8 + kt - 1))
                last = None
                for c in range(4):
                    t.ldweights(wkvbn[:, (h % 2) * 512 + c * 128:(h % 2) * 512 + (c + 1) * 128])
                    last = t.matmul(pb[kt % 2][0:128, :],
                                    wkvbn[:, (h % 2) * 512 + c * 128:(h % 2) * 512 + (c + 1) * 128],
                                    latf[:, c * S + kt * 512: c * S + (kt + 1) * 512],
                                    start=(c == 0), stop=(c == 3))
                last.then_inc(b.sem["tm"], 1)
        b.val["tm"] += NH * 8
        for kc in range(32):
            bs = 4 * (kc % 2)
            t.wait_ge(b.sem["dv"], dv3 + 128 + max(0, kc - 1))
            last = None
            for c in range(4):
                t.ldweights(latf[:, c * S + kc * 128: c * S + (kc + 1) * 128])
                for vg in range(4):
                    last = t.matmul(pb[bs + vg][0:128, :], latf[:, c * S + kc * 128: c * S + (kc + 1) * 128],
                                    wkvbv[:, c * 2048 + vg * 512: c * 2048 + (vg + 1) * 512],
                                    start=(c == 0), stop=(c == 3))
            last.then_inc(b.sem["tm"], 1)
        b.val["tm"] += 32
    blk.tensor(ftm3)

    def fvec5(v):
        for h in range(NH):
            for kt in range(8):
                idx = h * 8 + kt
                v.wait_ge(b.sem["tm"], tm3 + idx + 1)
                if idx >= 2:
                    v.wait_ge(b.sem["sE" if kt % 2 == 0 else "sF"],
                              (sE1 if kt % 2 == 0 else sF1) + 16 * (idx // 2))
                v.tensor_copy(out=keb[:, (kt % 2) * 512:(kt % 2 + 1) * 512], in_=pb[kt % 2][0:128, :]).then_inc(b.sem["dv"], 1)
        b.val["dv"] += NH * 8
        for kc in range(32):
            bs = 4 * (kc % 2)
            v.wait_ge(b.sem["tm"], tm3 + 128 + kc + 1)
            if kc >= 2:
                v.wait_ge(b.sem["sE" if kc % 2 == 0 else "sF"],
                          (sE1 if kc % 2 == 0 else sF1) + 16 * 64 + 16 * (kc // 2))
            for vg in range(4):
                op = v.tensor_copy(out=veb[:, (kc % 2) * 2048 + vg * 512:(kc % 2) * 2048 + (vg + 1) * 512],
                                   in_=pb[bs + vg][0:128, :])
            op.then_inc(b.sem["dv"], 1)
        b.val["dv"] += 32
    blk.vector(fvec5)

    def fgp2(g):
        for h in range(NH):
            for kt in range(8):
                g.wait_ge(b.sem["dv"], dv3 + h * 8 + kt + 1)
                sem = "sE" if kt % 2 == 0 else "sF"
                g.dma_start(out=io["kexp"][h * 128:(h + 1) * 128, kt * 512:(kt + 1) * 512],
                            in_=keb[:, (kt % 2) * 512:(kt % 2 + 1) * 512]).then_inc(b.sem[sem], 16)
                b.val[sem] += 16
        for kc in range(32):
            g.wait_ge(b.sem["dv"], dv3 + 128 + kc + 1)
            sem = "sE" if kc % 2 == 0 else "sF"
            g.dma_start(out=io["vexp"][kc * 128:(kc + 1) * 128, :], in_=veb[:, (kc % 2) * 2048:(kc % 2 + 1) * 2048]).then_inc(b.sem[sem], 16)
            b.val[sem] += 16
        g.wait_ge(b.sem["sE"], b.val["sE"])
        g.wait_ge(b.sem["sF"], b.val["sF"])
    blk.gpsimd(fgp2)
    barrier(b, blk)


def _attn1_kv(nc, b, blk, io, P, pb, xn):
    b.arena_reset()
    wst = b.tile("wst", [128, 2 * D], BF16)
    kraw = b.tile("kraw", [128, 2 * 512], BF16)
    ropt = b.tile("ropt", [128, 2 * 512], F32)
    ropt2 = b.tile("ropt2", [128, 2 * 512], F32)
    krotb = b.tile("krotb", [128, 2 * 512], BF16)
    cs1 = b.tile("cs1", [128, T], F32)
    sn1 = b.tile("sn1", [128, T], F32)
    veb = b.tile("veb", [128, 2 * 512], BF16)
    wvst = b.tile("wvst", [128, 2 * 16 * 512], BF16)

    sA0, sC0, sD0 = b.val["sA"], b.val["sC"], b.val["sD"]
    tm0, ac0, dv0 = b.val["tm"], b.val["ac"], b.val["dv"]
    sE0, sF0 = b.val["sE"], b.val["sF"]
    wk_r = io["wkT_b"].rearrange("(c p) n -> p c n", p=128)
    wv_r = io["wvT_b"].rearrange("(c p) n -> p c n", p=128)

    def fsync(s):
        s.dma_start(out=cs1[:], in_=io["cos1"]).then_inc(b.sem["sA"], 16)
        s.dma_start(out=sn1[:], in_=io["sin1"]).then_inc(b.sem["sA"], 16)
        b.val["sA"] += 32
        for hc in range(NH):
            if hc >= 2:
                s.wait_ge(b.sem["tm"], tm0 + 8 * (hc - 1))
            sem = "sC" if hc % 2 == 0 else "sD"
            s.dma_start(out=wst[:, (hc % 2) * D:(hc % 2 + 1) * D].rearrange("p (c n) -> p c n", n=128),
                        in_=wk_r[:, :, hc * 128:(hc + 1) * 128]).then_inc(b.sem[sem], 16)
            b.val[sem] += 16
    blk.sync(fsync)

    def ftm(t):
        t.wait_ge(b.sem["sA"], sA0 + 32)
        for hc in range(NH):
            t.wait_ge(b.sem["sC" if hc % 2 == 0 else "sD"], (sC0 if hc % 2 == 0 else sD0) + 16 * (hc // 2 + 1))
            for tt in range(4):
                idx = hc * 4 + tt
                if idx >= 2:
                    t.wait_ge(b.sem["ac"], ac0 + idx - 1)
                    t.wait_ge(b.sem["dv"], dv0 + 3 * (idx - 1))
                last = None
                for c in range(DC):
                    t.ldweights(wst[:, (hc % 2) * D + c * 128:(hc % 2) * D + (c + 1) * 128])
                    last = t.matmul(pb[idx % 2][0:128, :],
                                    wst[:, (hc % 2) * D + c * 128:(hc % 2) * D + (c + 1) * 128],
                                    xn[:, c * T + tt * 512: c * T + (tt + 1) * 512],
                                    start=(c == 0), stop=(c == DC - 1))
                last.then_inc(b.sem["tm"], 1)
                t.wait_ge(b.sem["ac"], ac0 + idx + 1)
                t.ldweights(P["R128T"][:])
                t.matmul(pb[2 + idx % 2][0:128, :], P["R128T"][:], kraw[:, (idx % 2) * 512:(idx % 2 + 1) * 512],
                         start=True, stop=True).then_inc(b.sem["tm"], 1)
        b.val["tm"] += NH * 8
    blk.tensor(ftm)

    def fact(sc):
        for hc in range(NH):
            for tt in range(4):
                idx = hc * 4 + tt
                sc.wait_ge(b.sem["tm"], tm0 + 2 * idx + 1)
                if idx >= 2:
                    sc.wait_ge(b.sem["dv"], dv0 + 3 * (idx - 1))
                sc.activation(out=kraw[:, (idx % 2) * 512:(idx % 2 + 1) * 512], in_=pb[idx % 2][0:128, :],
                              func=AF.Identity, bias=P["bk"][:, hc:hc + 1]).then_inc(b.sem["ac"], 1)
        b.val["ac"] += NH * 4
    blk.scalar(fact)

    def fvec(v):
        for hc in range(NH):
            for tt in range(4):
                idx = hc * 4 + tt
                v.wait_ge(b.sem["tm"], tm0 + 2 * idx + 2)
                if idx >= 2:
                    v.wait_ge(b.sem["sE" if idx % 2 == 0 else "sF"],
                              (sE0 if idx % 2 == 0 else sF0) + 16 * (idx // 2))
                sl = (idx % 2) * 512
                v.tensor_tensor(out=ropt[:, sl:sl + 512], in0=kraw[:, sl:sl + 512],
                                in1=cs1[:, tt * 512:(tt + 1) * 512], op=ALU.mult).then_inc(b.sem["dv"], 1)
                v.scalar_tensor_tensor(out=ropt2[:, sl:sl + 512], in0=pb[2 + idx % 2][0:128, :], scalar=1.0,
                                       in1=sn1[:, tt * 512:(tt + 1) * 512], op0=ALU.mult, op1=ALU.mult).then_inc(b.sem["dv"], 1)
                v.tensor_tensor(out=krotb[:, sl:sl + 512], in0=ropt[:, sl:sl + 512],
                                in1=ropt2[:, sl:sl + 512], op=ALU.add).then_inc(b.sem["dv"], 1)
        b.val["dv"] += NH * 4 * 3
    blk.vector(fvec)

    def fgp(g):
        for hc in range(NH):
            for tt in range(4):
                idx = hc * 4 + tt
                g.wait_ge(b.sem["dv"], dv0 + 3 * (idx + 1))
                sem = "sE" if idx % 2 == 0 else "sF"
                g.dma_start(out=io["kT_loc"][hc * 128:(hc + 1) * 128, tt * 512:(tt + 1) * 512],
                            in_=krotb[:, (idx % 2) * 512:(idx % 2 + 1) * 512]).then_inc(b.sem[sem], 16)
                b.val[sem] += 16
    blk.gpsimd(fgp)
    barrier(b, blk)

    tm1, dv1 = b.val["tm"], b.val["dv"]
    sC1, sD1 = b.val["sC"], b.val["sD"]
    sE1, sF1 = b.val["sE"], b.val["sF"]

    def fsync2(s):
        for fg in range(4):
            if fg >= 2:
                s.wait_ge(b.sem["tm"], tm1 + 16 * (fg - 1))
            sem = "sC" if fg % 2 == 0 else "sD"
            s.dma_start(out=wvst[:, (fg % 2) * 16 * 512:(fg % 2 + 1) * 16 * 512].rearrange("p (c n) -> p c n", n=512),
                        in_=wv_r[:, :, fg * 512:(fg + 1) * 512]).then_inc(b.sem[sem], 16)
            b.val[sem] += 16
    blk.sync(fsync2)

    def ftm2(t):
        for fg in range(4):
            t.wait_ge(b.sem["sC" if fg % 2 == 0 else "sD"], (sC1 if fg % 2 == 0 else sD1) + 16 * (fg // 2 + 1))
            for tc in range(16):
                idx = fg * 16 + tc
                if idx >= 2:
                    t.wait_ge(b.sem["dv"], dv1 + (idx - 1))
                for c in range(DC):
                    t.ldweights(xn[:, c * T + tc * 128: c * T + (tc + 1) * 128])
                    t.matmul(pb[idx % 2][0:128, :], xn[:, c * T + tc * 128: c * T + (tc + 1) * 128],
                             wvst[:, (fg % 2) * 16 * 512 + c * 512:(fg % 2) * 16 * 512 + (c + 1) * 512],
                             start=(c == 0), stop=False)
                t.ldweights(P["ones"][:])
                t.matmul(pb[idx % 2][0:128, :], P["ones"][:], P["bv"][:, fg * 512:(fg + 1) * 512],
                         start=False, stop=True).then_inc(b.sem["tm"], 1)
        b.val["tm"] += 64
    blk.tensor(ftm2)

    def fvec2(v):
        for fg in range(4):
            for tc in range(16):
                idx = fg * 16 + tc
                v.wait_ge(b.sem["tm"], tm1 + idx + 1)
                if idx >= 2:
                    v.wait_ge(b.sem["sE" if idx % 2 == 0 else "sF"],
                              (sE1 if idx % 2 == 0 else sF1) + 16 * (idx // 2))
                v.tensor_copy(out=veb[:, (idx % 2) * 512:(idx % 2 + 1) * 512], in_=pb[idx % 2][0:128, :]).then_inc(b.sem["dv"], 1)
        b.val["dv"] += 64
    blk.vector(fvec2)

    def fgp2(g):
        for fg in range(4):
            for tc in range(16):
                idx = fg * 16 + tc
                g.wait_ge(b.sem["dv"], dv1 + idx + 1)
                sem = "sE" if idx % 2 == 0 else "sF"
                g.dma_start(out=io["v_loc"][tc * 128:(tc + 1) * 128, fg * 512:(fg + 1) * 512],
                            in_=veb[:, (idx % 2) * 512:(idx % 2 + 1) * 512]).then_inc(b.sem[sem], 16)
                b.val[sem] += 16
        g.wait_ge(b.sem["sE"], b.val["sE"])
        g.wait_ge(b.sem["sF"], b.val["sF"])
        g.collective_compute("AllGather", ALU.bypass, replica_groups=[[0, 1], [2, 3], [4, 5], [6, 7]],
                             ins=[io["kT_loc"]], outs=[io["kT_all"]]).then_inc(b.sem["cc"], 1)
        g.collective_compute("AllGather", ALU.bypass, replica_groups=[[0, 1], [2, 3], [4, 5], [6, 7]],
                             ins=[io["v_loc"]], outs=[io["v_all"]]).then_inc(b.sem["cc"], 1)
        b.val["cc"] += 2
        g.wait_ge(b.sem["cc"], b.val["cc"])
    blk.gpsimd(fgp2)
    barrier(b, blk)


def _attn_core(nc, b, blk, io, P, pb, xn, layer, xsrc, xdst):
    b.arena_reset()
    qn_off = (b.arena_off + 63) // 64 * 64
    qn = b.tile("qn", [128, 16 * 512], BF16)
    qpe = b.tile("qpe", [128, 8 * 512], BF16)
    kh_off = (b.arena_off + 63) // 64 * 64
    kh = b.tile("kh", [128, 2 * S], BF16)
    vh = b.tile("vh", [128, 2 * S], BF16)
    rop1 = nc.alloc_sbuf_tensor_at(f"A{b.uid()}_rop1", [128, 2 * 512], F32, offset=kh_off)
    rop2 = nc.alloc_sbuf_tensor_at(f"A{b.uid()}_rop2", [128, 2 * 512], F32, offset=kh_off + 4096)
    kpd = b.tile("kpd", [128, S], BF16)
    pt = b.tile("pt", [128, 4 * 512], BF16)
    ao_off = (b.arena_off + 63) // 64 * 64
    ao = b.tile("ao", [128, 16 * 512], BF16)
    rs = b.tile("rs", [1, 512], F32)
    rsb = b.tile("rsb", [1, 512], BF16)
    rb32 = b.tile("rb32", [128, 512], F32)
    wq = b.tile("wq", [128, 2 * D], BF16)
    qab = nc.alloc_sbuf_tensor_at(f"A{b.uid()}_qab", [128, 12 * 512], BF16, offset=ao_off)
    sqb = b.tile("sqb", [128, 2 * 512], BF16)
    qraw = b.tile("qraw", [128, 2 * 512], BF16)
    cs = b.tile("cs", [128, 512], F32)
    sn = b.tile("sn", [128, 512], F32)
    xr = b.tile("xr", [128, 2 * 512], F32)
    xnw = b.tile("xnw", [128, 2 * 512], F32)

    scale = SC0 if layer == 0 else SC1
    k_src = io["kexp"] if layer == 0 else io["kT_all"]
    v_src = io["vexp"] if layer == 0 else io["v_all"]
    v_r = v_src.rearrange("(kc p) n -> p kc n", p=128)
    wo_h = io["wo_b"] if layer == 0 else io["woT_b"]
    wo_r = wo_h.rearrange("(c p) n -> p c n", p=128)
    costab = io["cos0d"] if layer == 0 else io["cos1"]
    sintab = io["sin0d"] if layer == 0 else io["sin1"]

    for qt in range(4):
        if layer == 0:
            _q_prod_l0(nc, b, blk, io, P, pb, xn, qt, qn, qpe, wq, qab, sqb, qraw,
                       rop1, rop2, cs, sn, rs, rsb, rb32, costab, sintab)
            if qt == 0:
                sA2 = b.val["sA"]

                def fsk(s, sA2=sA2):
                    s.dma_start(out=kpd[0:64, 0:T], in_=io["lat_all"][512:576, :]).then_inc(b.sem["sA"], 16)
                    s.dma_start(out=kpd[64:128, 0:T], in_=io["lat_all"][512:576, :]).then_inc(b.sem["sA"], 16)
                    s.dma_start(out=kpd[0:64, T:S], in_=io["lat_all"][1088:1152, :]).then_inc(b.sem["sA"], 16)
                    s.dma_start(out=kpd[64:128, T:S], in_=io["lat_all"][1088:1152, :]).then_inc(b.sem["sA"], 16)
                    b.val["sA"] += 64
                blk.sync(fsk)
        else:
            _q_prod_l1(nc, b, blk, io, P, pb, xn, qt, qn, wq, qraw, rop1, rop2, cs, sn, costab, sintab)

        barrier(b, blk)
        if ATTN_PARTS == "qprod":
            continue

        M = sum(1 for kc in range(32) if layer == 0 and _cls_l0(qt, kc)[1] is not None)
        sC2, sD2 = b.val["sC"], b.val["sD"]
        tm4, ac4, dv6 = b.val["tm"], b.val["ac"], b.val["dv"]
        nload = 48 if layer == 1 else 32

        def fs3(s, sC2=sC2, sD2=sD2, tm4=tm4):
            for h in range(NH):
                if h >= 2:
                    s.wait_ge(b.sem["tm"], tm4 + 65 * (h - 1))
                sem = "sC" if h % 2 == 0 else "sD"
                sl = (h % 2) * S
                if layer == 0:
                    s.dma_start(out=kh[:, sl:sl + S], in_=k_src[h * 128:(h + 1) * 128, :]).then_inc(b.sem[sem], 16)
                else:
                    s.dma_start(out=kh[:, sl:sl + T], in_=k_src[h * 128:(h + 1) * 128, :]).then_inc(b.sem[sem], 16)
                    s.dma_start(out=kh[:, sl + T:sl + S], in_=k_src[D + h * 128: D + (h + 1) * 128, :]).then_inc(b.sem[sem], 16)
                s.dma_start(out=vh[:, sl:sl + S].rearrange("p (kc n) -> p kc n", n=128),
                            in_=v_r[:, :, h * 128:(h + 1) * 128]).then_inc(b.sem[sem], 16)
                b.val[sem] += nload
        blk.sync(fs3)

        def ft5(t, sC2=sC2, sD2=sD2, tm4=tm4, ac4=ac4, dv6=dv6, qt=qt, M=M):
            for h in range(NH):
                t.wait_ge(b.sem["sC" if h % 2 == 0 else "sD"], (sC2 if h % 2 == 0 else sD2) + nload * (h // 2 + 1))
                if h >= 1:
                    t.wait_ge(b.sem["dv"], dv6 + h * (M + 2))
                sl = (h % 2) * S
                for kc in range(32):
                    if kc >= 2:
                        t.wait_ge(b.sem["ac"], ac4 + 32 * h + kc - 1)
                    t.ldweights(kh[:, sl + kc * 128: sl + (kc + 1) * 128])
                    m1 = t.matmul(pb[kc % 2][0:128, :], kh[:, sl + kc * 128: sl + (kc + 1) * 128],
                                  qn[:, h * 512:(h + 1) * 512], start=True, stop=(layer == 1))
                    if layer == 0:
                        hp = (h % 2) * 64
                        tp = (hp, 0)
                        t.ldweights(kpd[hp:hp + 64, kc * 128:(kc + 1) * 128], tile_position=tp)
                        m1 = t.matmul(pb[kc % 2][0:128, :], kpd[hp:hp + 64, kc * 128:(kc + 1) * 128],
                                      qpe[hp:hp + 64, (h // 2) * 512:(h // 2 + 1) * 512], start=False, stop=True,
                                      tile_position=tp)
                    m1.then_inc(b.sem["tm"], 1)
                    if kc >= 3:
                        kcp = kc - 3
                        t.wait_ge(b.sem["ac"], ac4 + 32 * h + kcp + 1)
                        t.ldweights(P["onesc"][:])
                        t.matmul(pb[6][0:1, :], P["onesc"][:], pt[:, (kcp % 4) * 512:(kcp % 4 + 1) * 512],
                                 start=(kcp == 0), stop=(kcp == 31))
                        t.ldweights(vh[:, sl + kcp * 128: sl + (kcp + 1) * 128])
                        t.matmul(pb[7][0:128, :], vh[:, sl + kcp * 128: sl + (kcp + 1) * 128],
                                 pt[:, (kcp % 4) * 512:(kcp % 4 + 1) * 512],
                                 start=(kcp == 0), stop=(kcp == 31)).then_inc(b.sem["tm"], 1)
                for kcp in range(29, 32):
                    t.wait_ge(b.sem["ac"], ac4 + 32 * h + kcp + 1)
                    t.ldweights(P["onesc"][:])
                    t.matmul(pb[6][0:1, :], P["onesc"][:], pt[:, (kcp % 4) * 512:(kcp % 4 + 1) * 512],
                             start=(kcp == 0), stop=(kcp == 31))
                    t.ldweights(vh[:, sl + kcp * 128: sl + (kcp + 1) * 128])
                    t.matmul(pb[7][0:128, :], vh[:, sl + kcp * 128: sl + (kcp + 1) * 128],
                             pt[:, (kcp % 4) * 512:(kcp % 4 + 1) * 512],
                             start=(kcp == 0), stop=(kcp == 31)).then_inc(b.sem["tm"], 1)
                t.wait_ge(b.sem["dv"], dv6 + h * (M + 2) + M + 1)
                t.ldweights(P["ones"][:])
                t.matmul(pb[2][0:128, :], P["ones"][:], rsb[:], start=True, stop=True).then_inc(b.sem["tm"], 1)
            b.val["tm"] += NH * 65
        blk.tensor(ft5)

        def fa3(sc, tm4=tm4, ac4=ac4, dv6=dv6, qt=qt, M=M):
            for h in range(NH):
                nm = 0
                for kc in range(32):
                    sc.wait_ge(b.sem["tm"], tm4 + 65 * h + _off_scores(kc))
                    if kc >= 4:
                        sc.wait_ge(b.sem["tm"], tm4 + 65 * h + _off_pv(kc - 4))
                    bias_kind, mi = _cls_l0(qt, kc) if layer == 0 else ("allow", None)
                    if mi is not None:
                        nm += 1
                        sc.wait_ge(b.sem["dv"], dv6 + h * (M + 2) + nm)
                    bias = P["ballow"][:] if bias_kind == "allow" else (P["bneg"][:] if bias_kind == "neg" else P["bmd"][:])
                    sc.activation(out=pt[:, (kc % 4) * 512:(kc % 4 + 1) * 512], in_=pb[kc % 2][0:128, :],
                                  func=AF.Exp, bias=bias, scale=scale).then_inc(b.sem["ac"], 1)
            b.val["ac"] += NH * 32
        blk.scalar(fa3)

        def fv7(v, tm4=tm4, dv6=dv6, qt=qt, M=M):
            for h in range(NH):
                nm = 0
                if layer == 0:
                    for kc in range(32):
                        bias_kind, mi = _cls_l0(qt, kc)
                        if mi is not None:
                            v.wait_ge(b.sem["tm"], tm4 + 65 * h + _off_scores(kc))
                            v.tensor_tensor(out=pb[kc % 2][0:128, :], in0=pb[kc % 2][0:128, :],
                                            in1=P["masks8"][:, mi * 512:(mi + 1) * 512], op=ALU.add).then_inc(b.sem["dv"], 1)
                            nm += 1
                v.wait_ge(b.sem["tm"], tm4 + 65 * h + 64)
                v.reciprocal(out=rs[:], in_=pb[6][0:1, :])
                v.tensor_copy(out=rsb[:], in_=rs[:]).then_inc(b.sem["dv"], 1)
                v.wait_ge(b.sem["tm"], tm4 + 65 * (h + 1))
                v.tensor_copy(out=rb32[:], in_=pb[2][0:128, :])
                v.scalar_tensor_tensor(out=ao[:, h * 512:(h + 1) * 512], in0=pb[7][0:128, :], scalar=1.0,
                                       in1=rb32[:], op0=ALU.mult, op1=ALU.mult).then_inc(b.sem["dv"], 1)
                b.val["dv"] += nm + 2
        blk.vector(fv7)

        barrier(b, blk)
        if ATTN_PARTS == "heads":
            continue

        sC3, sD3, sG0, sA9 = b.val["sC"], b.val["sD"], b.val["sG"], b.val["sA"]
        sH0, sE9, sF9 = b.val["sH"], b.val["sE"], b.val["sF"]
        tm5, dv7 = b.val["tm"], b.val["dv"]

        def fs4(s, sC3=sC3, sD3=sD3, tm5=tm5, dv7=dv7, qt=qt):
            for dout in range(16):
                if dout >= 2:
                    s.wait_ge(b.sem["tm"], tm5 + (dout - 1))
                sem = "sC" if dout % 2 == 0 else "sD"
                s.dma_start(out=wq[:, (dout % 2) * D:(dout % 2 + 1) * D].rearrange("p (c n) -> p c n", n=128),
                            in_=wo_r[:, :, dout * 128:(dout + 1) * 128]).then_inc(b.sem[sem], 16)
                b.val[sem] += 16
                if ATTN_PARTS == "oprojnx":
                    continue
                if dout >= 2:
                    s.wait_ge(b.sem["dv"], dv7 + (dout - 1))
                xsem = "sG" if dout % 2 == 0 else "sA"
                s.dma_start(out=xr[:, (dout % 2) * 512:(dout % 2 + 1) * 512],
                            in_=xsrc[dout * 128:(dout + 1) * 128, qt * 512:(qt + 1) * 512]).then_inc(b.sem[xsem], 16)
                b.val[xsem] += 16
        blk.sync(fs4)

        def ft6(t, sC3=sC3, sD3=sD3, tm5=tm5, dv7=dv7):
            for dout in range(16):
                t.wait_ge(b.sem["sC" if dout % 2 == 0 else "sD"], (sC3 if dout % 2 == 0 else sD3) + 16 * (dout // 2 + 1))
                if dout >= 2:
                    t.wait_ge(b.sem["dv"], dv7 + (dout - 1))
                last = None
                for hc in range(NH):
                    t.ldweights(wq[:, (dout % 2) * D + hc * 128:(dout % 2) * D + (hc + 1) * 128])
                    last = t.matmul(pb[dout % 2][0:128, :], wq[:, (dout % 2) * D + hc * 128:(dout % 2) * D + (hc + 1) * 128],
                                    ao[:, hc * 512:(hc + 1) * 512], start=(hc == 0), stop=(hc == NH - 1))
                last.then_inc(b.sem["tm"], 1)
            b.val["tm"] += 16
        blk.tensor(ft6)

        def fv8(v, tm5=tm5, dv7=dv7, sG0=sG0, sA9=sA9, sH0=sH0, sE9=sE9):
            for dout in range(16):
                v.wait_ge(b.sem["tm"], tm5 + dout + 1)
                if dout >= 2:
                    ssem, sbase = ("sH", sH0) if dout % 2 == 0 else ("sE", sE9)
                    v.wait_ge(b.sem[ssem], sbase + 16 * (dout // 2))
                if ATTN_PARTS == "oprojnx":
                    v.tensor_copy(out=xnw[:, (dout % 2) * 512:(dout % 2 + 1) * 512],
                                  in_=pb[dout % 2][0:128, :]).then_inc(b.sem["dv"], 1)
                    continue
                xsem, xbase = ("sG", sG0) if dout % 2 == 0 else ("sA", sA9)
                v.wait_ge(b.sem[xsem], xbase + 16 * (dout // 2 + 1))
                if ATTN_PARTS == "oprojx":
                    v.tensor_copy(out=xnw[:, (dout % 2) * 512:(dout % 2 + 1) * 512],
                                  in_=pb[dout % 2][0:128, :]).then_inc(b.sem["dv"], 1)
                    continue
                if OPROJ_IN1 == "copy":
                    v.tensor_copy(out=xnw[:, (dout % 2) * 512:(dout % 2 + 1) * 512],
                                  in_=pb[dout % 2][0:128, :]).then_inc(b.sem["dv"], 1)
                    continue
                if OPROJ_IN1 == "swap":
                    v.scalar_tensor_tensor(out=xnw[:, (dout % 2) * 512:(dout % 2 + 1) * 512],
                                           in0=xr[:, (dout % 2) * 512:(dout % 2 + 1) * 512],
                                           scalar=1.0, in1=pb[dout % 2][0:128, :],
                                           op0=ALU.mult, op1=ALU.add).then_inc(b.sem["dv"], 1)
                    continue
                in1 = (rb32[0:128, 0:512] if OPROJ_IN1 == "rb32" else
                       (qn[:, 0:512] if OPROJ_IN1 == "qnbf" else
                        xr[:, (dout % 2) * 512:(dout % 2 + 1) * 512]))
                v.scalar_tensor_tensor(out=xnw[:, (dout % 2) * 512:(dout % 2 + 1) * 512], in0=pb[dout % 2][0:128, :],
                                       scalar=1.0, in1=in1,
                                       op0=ALU.mult, op1=ALU.add).then_inc(b.sem["dv"], 1)
            b.val["dv"] += 16
        blk.vector(fv8)

        def fg4(g, dv7=dv7, qt=qt):
            for dout in range(16):
                g.wait_ge(b.sem["dv"], dv7 + dout + 1)
                ssem = "sH" if dout % 2 == 0 else "sE"
                g.dma_start(out=xdst[dout * 128:(dout + 1) * 128, qt * 512:(qt + 1) * 512],
                            in_=xnw[:, (dout % 2) * 512:(dout % 2 + 1) * 512]).then_inc(b.sem[ssem], 16)
                b.val[ssem] += 16
            g.wait_ge(b.sem["sH"], b.val["sH"])
            g.wait_ge(b.sem["sE"], b.val["sE"])
        blk.gpsimd(fg4)

        barrier(b, blk)


def _q_prod_l0(nc, b, blk, io, P, pb, xn, qt, qn, qpe, wq, qab, sqb, qraw,
               rop1, rop2, cs, sn, rs, rsb, rb32, costab, sintab):
    sA0, sC0, sD0 = b.val["sA"], b.val["sC"], b.val["sD"]
    tm0, ac0, dv0 = b.val["tm"], b.val["ac"], b.val["dv"]
    wqa_r = io["wqa_b"].rearrange("(c p) n -> p c n", p=128)
    wqb_r = io["wqb_b"].rearrange("(c p) n -> p c n", p=128)

    def fs1(s):
        s.dma_start(out=cs[:], in_=costab[:, qt * 512:(qt + 1) * 512]).then_inc(b.sem["sA"], 16)
        s.dma_start(out=sn[:], in_=sintab[:, qt * 512:(qt + 1) * 512]).then_inc(b.sem["sA"], 16)
        b.val["sA"] += 32
        for fc in range(12):
            if fc >= 2:
                s.wait_ge(b.sem["tm"], tm0 + 2 * (fc - 1))
            sem = "sC" if fc % 2 == 0 else "sD"
            s.dma_start(out=wq[:, (fc % 2) * D:(fc % 2 + 1) * D].rearrange("p (c n) -> p c n", n=128),
                        in_=wqa_r[:, :, fc * 128:(fc + 1) * 128]).then_inc(b.sem[sem], 16)
            b.val[sem] += 16
    blk.sync(fs1)

    def ft1(t):
        for fc in range(12):
            t.wait_ge(b.sem["sC" if fc % 2 == 0 else "sD"], (sC0 if fc % 2 == 0 else sD0) + 16 * (fc // 2 + 1))
            if fc >= 2:
                t.wait_ge(b.sem["dv"], dv0 + (fc - 1))
                t.wait_ge(b.sem["ac"], ac0 + (fc - 1))
            last = None
            for c in range(DC):
                t.ldweights(wq[:, (fc % 2) * D + c * 128:(fc % 2) * D + (c + 1) * 128])
                last = t.matmul(pb[fc % 2][0:128, :], wq[:, (fc % 2) * D + c * 128:(fc % 2) * D + (c + 1) * 128],
                                xn[:, c * T + qt * 512: c * T + (qt + 1) * 512],
                                start=(c == 0), stop=(c == DC - 1))
            last.then_inc(b.sem["tm"], 1)
            t.wait_ge(b.sem["ac"], ac0 + fc + 1)
            t.ldweights(P["onesc"][:])
            t.matmul(pb[7][0:1, :], P["onesc"][:], sqb[:, (fc % 2) * 512:(fc % 2 + 1) * 512],
                     start=(fc == 0), stop=(fc == 11)).then_inc(b.sem["tm"], 1)
        b.val["tm"] += 24
    blk.tensor(ft1)

    def fa1(sc):
        for fc in range(12):
            sc.wait_ge(b.sem["tm"], tm0 + 2 * fc + 1)
            if fc >= 2:
                sc.wait_ge(b.sem["tm"], tm0 + 2 * (fc - 1))
            sc.activation(out=sqb[:, (fc % 2) * 512:(fc % 2 + 1) * 512], in_=pb[fc % 2][0:128, :],
                          func=AF.Square).then_inc(b.sem["ac"], 1)
        b.val["ac"] += 12
    blk.scalar(fa1)

    def fv1(v):
        for fc in range(12):
            v.wait_ge(b.sem["tm"], tm0 + 2 * fc + 1)
            v.wait_ge(b.sem["ac"], ac0 + fc + 1)
            v.tensor_copy(out=qab[:, fc * 512:(fc + 1) * 512], in_=pb[fc % 2][0:128, :]).then_inc(b.sem["dv"], 1)
        b.val["dv"] += 12
    blk.vector(fv1)

    tm1 = b.val["tm"]

    def fv2(v):
        v.wait_ge(b.sem["tm"], tm1)
        v.tensor_scalar(out=rs[:], in0=pb[7][0:1, :], scalar1=1.0 / QAR, scalar2=1e-6,
                        op0=ALU.mult, op1=ALU.add).then_inc(b.sem["dv"], 1)
        b.val["dv"] += 1
    blk.vector(fv2)
    dvv = b.val["dv"]

    def fa2(sc):
        sc.wait_ge(b.sem["dv"], dvv)
        sc.activation(out=rs[:], in_=rs[:], func=AF.Sqrt).then_inc(b.sem["ac"], 1)
        b.val["ac"] += 1
    blk.scalar(fa2)
    acv = b.val["ac"]

    def fv3(v):
        v.wait_ge(b.sem["ac"], acv)
        v.reciprocal(out=rs[:], in_=rs[:])
        v.tensor_copy(out=rsb[:], in_=rs[:]).then_inc(b.sem["dv"], 1)
        b.val["dv"] += 1
    blk.vector(fv3)
    dvv2 = b.val["dv"]

    def ft2(t):
        t.wait_ge(b.sem["dv"], dvv2)
        t.ldweights(P["ones"][:])
        t.matmul(pb[6][0:128, :], P["ones"][:], rsb[:], start=True, stop=True).then_inc(b.sem["tm"], 1)
        b.val["tm"] += 1
    blk.tensor(ft2)
    tmv = b.val["tm"]

    def fv4(v):
        v.wait_ge(b.sem["tm"], tmv)
        v.tensor_copy(out=rb32[:], in_=pb[6][0:128, :])
        for fc in range(12):
            op = v.tensor_tensor(out=qab[:, fc * 512:(fc + 1) * 512], in0=qab[:, fc * 512:(fc + 1) * 512],
                                 in1=rb32[:], op=ALU.mult)
        op.then_inc(b.sem["dv"], 1)
        b.val["dv"] += 1
    blk.vector(fv4)
    dvv3 = b.val["dv"]
    sC1, sD1 = b.val["sC"], b.val["sD"]
    tm2, dv4 = b.val["tm"], b.val["dv"]

    def fs2(s):
        s.wait_ge(b.sem["dv"], dvv3)
        for fc2 in range(24):
            if fc2 >= 2:
                toff = (fc2 - 1) if fc2 - 1 < 16 else 17 + 2 * (fc2 - 1 - 16)
                s.wait_ge(b.sem["tm"], tm2 + toff)
            sem = "sC" if fc2 % 2 == 0 else "sD"
            s.dma_start(out=wq[:, (fc2 % 2) * D:(fc2 % 2) * D + 12 * 128].rearrange("p (c n) -> p c n", n=128),
                        in_=wqb_r[:, :, fc2 * 128:(fc2 + 1) * 128]).then_inc(b.sem[sem], 16)
            b.val[sem] += 16
    blk.sync(fs2)

    def ft3(t):
        for fc2 in range(24):
            t.wait_ge(b.sem["sC" if fc2 % 2 == 0 else "sD"], (sC1 if fc2 % 2 == 0 else sD1) + 16 * (fc2 // 2 + 1))
            if fc2 >= 2:
                voff = (fc2 - 1) if fc2 - 1 < 16 else 16 + 2 * (fc2 - 1 - 16) + 1
                t.wait_ge(b.sem["dv"], dv4 + voff)
            last = None
            for j in range(12):
                t.ldweights(wq[:, (fc2 % 2) * D + j * 128:(fc2 % 2) * D + (j + 1) * 128])
                last = t.matmul(pb[fc2 % 2][0:128, :], wq[:, (fc2 % 2) * D + j * 128:(fc2 % 2) * D + (j + 1) * 128],
                                qab[:, j * 512:(j + 1) * 512], start=(j == 0), stop=(j == 11))
            last.then_inc(b.sem["tm"], 1)
            if fc2 >= 16:
                pc = fc2 - 16
                t.wait_ge(b.sem["dv"], dv4 + 16 + 2 * pc + 1)
                if pc >= 2:
                    t.wait_ge(b.sem["dv"], dv4 + 16 + 2 * (pc - 2) + 2)
                t.ldweights(P["R64T"][:])
                t.matmul(pb[4 + pc % 2][0:128, :], P["R64T"][:], qraw[:, (fc2 % 2) * 512:(fc2 % 2 + 1) * 512],
                         start=True, stop=True).then_inc(b.sem["tm"], 1)
        b.val["tm"] += 32
    blk.tensor(ft3)

    def fv5(v):
        for fc2 in range(24):
            if fc2 < 16:
                v.wait_ge(b.sem["tm"], tm2 + fc2 + 1)
                v.tensor_copy(out=qn[:, fc2 * 512:(fc2 + 1) * 512], in_=pb[fc2 % 2][0:128, :]).then_inc(b.sem["dv"], 1)
            else:
                pc = fc2 - 16
                v.wait_ge(b.sem["tm"], tm2 + 17 + 2 * pc)
                v.tensor_copy(out=qraw[:, (fc2 % 2) * 512:(fc2 % 2 + 1) * 512], in_=pb[fc2 % 2][0:128, :]).then_inc(b.sem["dv"], 1)
                v.wait_ge(b.sem["tm"], tm2 + 18 + 2 * pc)
                sl = (pc % 2) * 512
                v.tensor_tensor(out=rop1[:, sl:sl + 512], in0=qraw[:, (fc2 % 2) * 512:(fc2 % 2 + 1) * 512],
                                in1=cs[:], op=ALU.mult)
                v.scalar_tensor_tensor(out=rop2[:, sl:sl + 512], in0=pb[4 + pc % 2][0:128, :], scalar=1.0,
                                       in1=sn[:], op0=ALU.mult, op1=ALU.mult)
                v.tensor_tensor(out=qpe[:, pc * 512:(pc + 1) * 512], in0=rop1[:, sl:sl + 512],
                                in1=rop2[:, sl:sl + 512], op=ALU.add).then_inc(b.sem["dv"], 1)
        b.val["dv"] += 32
    blk.vector(fv5)


def _q_prod_l1(nc, b, blk, io, P, pb, xn, qt, qn, wq, qraw, rop1, rop2, cs, sn, costab, sintab):
    sA0, sC0, sD0 = b.val["sA"], b.val["sC"], b.val["sD"]
    tm0, ac0, dv0 = b.val["tm"], b.val["ac"], b.val["dv"]
    wq_r = io["wqT_b"].rearrange("(c p) n -> p c n", p=128)

    def fs1(s):
        s.dma_start(out=cs[:], in_=costab[:, qt * 512:(qt + 1) * 512]).then_inc(b.sem["sA"], 16)
        s.dma_start(out=sn[:], in_=sintab[:, qt * 512:(qt + 1) * 512]).then_inc(b.sem["sA"], 16)
        b.val["sA"] += 32
        for hc in range(16):
            if hc >= 2:
                s.wait_ge(b.sem["tm"], tm0 + 2 * (hc - 1))
            sem = "sC" if hc % 2 == 0 else "sD"
            s.dma_start(out=wq[:, (hc % 2) * D:(hc % 2 + 1) * D].rearrange("p (c n) -> p c n", n=128),
                        in_=wq_r[:, :, hc * 128:(hc + 1) * 128]).then_inc(b.sem[sem], 16)
            b.val[sem] += 16
    blk.sync(fs1)
    sA1 = b.val["sA"]

    def ft1(t):
        t.wait_ge(b.sem["sA"], sA1)
        for hc in range(16):
            t.wait_ge(b.sem["sC" if hc % 2 == 0 else "sD"], (sC0 if hc % 2 == 0 else sD0) + 16 * (hc // 2 + 1))
            if hc >= 2:
                t.wait_ge(b.sem["ac"], ac0 + hc - 1)
                t.wait_ge(b.sem["dv"], dv0 + 3 * (hc - 1))
            last = None
            for c in range(DC):
                t.ldweights(wq[:, (hc % 2) * D + c * 128:(hc % 2) * D + (c + 1) * 128])
                last = t.matmul(pb[hc % 2][0:128, :], wq[:, (hc % 2) * D + c * 128:(hc % 2) * D + (c + 1) * 128],
                                xn[:, c * T + qt * 512: c * T + (qt + 1) * 512],
                                start=(c == 0), stop=(c == DC - 1))
            last.then_inc(b.sem["tm"], 1)
            t.wait_ge(b.sem["ac"], ac0 + hc + 1)
            t.ldweights(P["R128T"][:])
            t.matmul(pb[4 + hc % 2][0:128, :], P["R128T"][:], qraw[:, (hc % 2) * 512:(hc % 2 + 1) * 512],
                     start=True, stop=True).then_inc(b.sem["tm"], 1)
        b.val["tm"] += 32
    blk.tensor(ft1)

    def fa1(sc):
        for hc in range(16):
            sc.wait_ge(b.sem["tm"], tm0 + 2 * hc + 1)
            if hc >= 2:
                sc.wait_ge(b.sem["dv"], dv0 + 3 * (hc - 1))
            sc.activation(out=qraw[:, (hc % 2) * 512:(hc % 2 + 1) * 512], in_=pb[hc % 2][0:128, :],
                          func=AF.Identity, bias=P["bq"][:, hc:hc + 1]).then_inc(b.sem["ac"], 1)
        b.val["ac"] += 16
    blk.scalar(fa1)

    def fv1(v):
        for hc in range(16):
            v.wait_ge(b.sem["tm"], tm0 + 2 * hc + 2)
            sl = (hc % 2) * 512
            v.tensor_tensor(out=rop1[:, sl:sl + 512], in0=qraw[:, sl:sl + 512], in1=cs[:],
                            op=ALU.mult).then_inc(b.sem["dv"], 1)
            v.scalar_tensor_tensor(out=rop2[:, sl:sl + 512], in0=pb[4 + hc % 2][0:128, :], scalar=1.0,
                                   in1=sn[:], op0=ALU.mult, op1=ALU.mult).then_inc(b.sem["dv"], 1)
            v.tensor_tensor(out=qn[:, hc * 512:(hc + 1) * 512], in0=rop1[:, sl:sl + 512],
                            in1=rop2[:, sl:sl + 512], op=ALU.add).then_inc(b.sem["dv"], 1)
        b.val["dv"] += 48
    blk.vector(fv1)


def emit_attn(nc, b, blk, io, P, pb, xn, layer, xsrc, xdst):
    if layer == 0:
        _attn0_kv(nc, b, blk, io, P, pb, xn)
    else:
        _attn1_kv(nc, b, blk, io, P, pb, xn)
    if ATTN_PARTS not in ("lat", "ag", "kv"):
        _attn_core(nc, b, blk, io, P, pb, xn, layer, xsrc, xdst)


def build_program(n_sub=None):
    from contextlib import ExitStack
    if n_sub is None:
        n_sub = N_SUB
    nc = bass.Bass("TRN2", target_bir_lowering=False, debug=False, detect_race_conditions=False)
    io = {}

    def ein(name, shape, dtype=F32):
        io[name] = nc.dram_tensor(name, shape, dtype, kind="ExternalInput").ap()

    ein("x0", [D, T])
    ein("gate_s", [256, I0]); ein("up_s", [256, I0]); ein("down_s", [768, D])
    ein("gsc", [128, 48]); ein("usc", [128, 48]); ein("dsc", [128, 16])
    ein("wqa_s", [256, QAR]); ein("wqb_s", [192, 3072]); ein("gqa_s", [192, 1])
    ein("wkva_s", [256, KVR + QP]); ein("wkvb_s", [64, 4096]); ein("gkva_s", [64, 1])
    ein("wo_s", [256, D])
    for nm in ("wqT_s", "wkT_s", "wvT_s", "woT_s"):
        ein(nm, [256, D])
    ein("w1T_s", [256, I1]); ein("w3T_s", [256, I1]); ein("w2T_s", [1024, D])
    ein("bq", [128, 16]); ein("bk", [128, 16]); ein("bv_row", [1, D], BF16)
    ein("ones_bf", [1, 128], BF16); ein("ones_col", [128, 1], BF16)
    ein("R64T", [128, 128], BF16); ein("R128T", [128, 128], BF16)
    ein("H128c", [128, 128], BF16); ein("H64c", [128, 128], BF16)
    ein("masks8", [8, 128, 512])
    ein("bmd", [128, 1])
    ein("cos0d", [128, T]); ein("sin0d", [128, T]); ein("cos1", [128, T]); ein("sin1", [128, T])

    xout = nc.dram_tensor("xout", [D, T], F32, kind="ExternalOutput").ap()
    io["xout"] = xout

    def scratch(name, shape, dtype=BF16):
        io[name] = nc.dram_tensor(name, shape, dtype, kind="Internal").ap()

    scratch("gate_q_loc", [256, I0]); scratch("gate_q", [D, I0])
    scratch("up_q_loc", [256, I0]); scratch("up_q", [D, I0])
    scratch("down_q_loc", [768, D]); scratch("down_q", [I0, D])
    scratch("wqa_loc", [256, QAR]); scratch("wqa_b", [D, QAR])
    scratch("wqb_loc", [192, 3072]); scratch("wqb_b", [QAR, 3072])
    scratch("wkva_loc", [256, KVR + QP]); scratch("wkva_b", [D, KVR + QP])
    scratch("wkvb_loc", [64, 4096]); scratch("wkvb_b", [KVR, 4096])
    scratch("wo_loc", [256, D]); scratch("wo_b", [D, D])
    for nm in ("wqT", "wkT", "wvT", "woT"):
        scratch(nm + "_loc", [256, D]); scratch(nm + "_b", [D, D])
    scratch("w1T_loc", [256, I1]); scratch("w1T_b", [D, I1])
    scratch("w3T_loc", [256, I1]); scratch("w3T_b", [D, I1])
    scratch("y2_loc", [1024, D]); scratch("y2_b", [I1, D])
    scratch("w2fT_b", [I1, D])
    scratch("hid", [I1, T])
    scratch("lat_loc", [576, T]); scratch("lat_all", [1152, T])
    scratch("kexp", [NH * 128, S])
    scratch("vexp", [S, D])
    scratch("kT_loc", [D, T]); scratch("kT_all", [2 * D, T])
    scratch("v_loc", [T, D]); scratch("v_all", [S, D])

    with ExitStack() as ctx:
        b = B(nc, ctx)
        P = {}
        P["ones"] = nc.alloc_sbuf_tensor("P_ones", [1, 128], BF16)
        P["onesc"] = nc.alloc_sbuf_tensor("P_onesc", [128, 1], BF16)
        P["gsc"] = nc.alloc_sbuf_tensor("P_gsc", [128, 48], F32)
        P["usc"] = nc.alloc_sbuf_tensor("P_usc", [128, 48], F32)
        P["dsc"] = nc.alloc_sbuf_tensor("P_dsc", [128, 16], F32)
        P["dsch"] = nc.alloc_sbuf_tensor("P_dsch", [128, 16], F32)
        P["bq"] = nc.alloc_sbuf_tensor("P_bq", [128, 16], F32)
        P["bk"] = nc.alloc_sbuf_tensor("P_bk", [128, 16], F32)
        P["bv"] = nc.alloc_sbuf_tensor("P_bv", [1, D], BF16)
        P["R64T"] = nc.alloc_sbuf_tensor("P_R64T", [128, 128], BF16)
        P["R128T"] = nc.alloc_sbuf_tensor("P_R128T", [128, 128], BF16)
        P["H128c"] = nc.alloc_sbuf_tensor("P_H128c", [128, 128], BF16)
        P["H64c"] = nc.alloc_sbuf_tensor("P_H64c", [128, 128], BF16)
        P["masks8"] = nc.alloc_sbuf_tensor("P_masks8", [128, 8 * 512], F32)
        P["bmd"] = nc.alloc_sbuf_tensor("P_bmd", [128, 1], F32)
        P["ballow"] = nc.alloc_sbuf_tensor("P_ballow", [128, 1], F32)
        P["bneg"] = nc.alloc_sbuf_tensor("P_bneg", [128, 1], F32)
        P["xn"] = nc.alloc_sbuf_tensor("P_xn", [128, DC * T], BF16)
        b.freeze_arena()

        pb = [ctx.enter_context(nc.psum_tensor(f"pb{i}", [128, 512], F32)) for i in range(8)]
        blk = ctx.enter_context(nc.Block())

        emit_consts(nc, b, blk, io, P)
        emit_prep(nc, b, blk, io, P, pb)
        barrier(b, blk)

        xn = P["xn"][:]
        seq = [("ffn", 0), ("attn", 0), ("ffn", 0), ("ffn", 1), ("attn", 1), ("ffn", 1)]
        xcur = io["x0"]
        for si in range(min(n_sub, 6)):
            kind, layer = seq[si]
            b.arena_reset()
            emit_ln(nc, b, blk, io, P, pb, xcur, xn)
            barrier(b, blk)
            if kind == "ffn":
                emit_ffn(nc, b, blk, io, P, pb, xn, layer, xsrc=xcur, xdst=io["xout"])
            else:
                emit_attn(nc, b, blk, io, P, pb, xn, layer, xsrc=xcur, xdst=io["xout"])
            barrier(b, blk)
            xcur = io["xout"]

        if n_sub == 0:
            b.arena_reset()
            t0 = b.tile("cp", [128, T], F32)
            sB0, sH0 = b.val["sB"], b.val["sH"]

            def fcp(g):
                for c in range(DC):
                    g.dma_start(out=t0[:], in_=io["x0"][c * 128:(c + 1) * 128, :]).then_inc(b.sem["sB"], 16)
                    g.wait_ge(b.sem["sB"], sB0 + 16 * (c + 1))
                    g.dma_start(out=io["xout"][c * 128:(c + 1) * 128, :], in_=t0[:]).then_inc(b.sem["sH"], 16)
                    g.wait_ge(b.sem["sH"], sH0 + 16 * (c + 1))
                b.val["sB"] += 16 * DC
                b.val["sH"] += 16 * DC
            blk.gpsimd(fcp)

        sH_fin = b.val["sH"]

        def ffin(g):
            g.wait_ge(b.sem["sH"], sH_fin)
        blk.gpsimd(ffin)

    return nc


def kernel(**inputs):
    in_maps = _host_layout(inputs)
    nc = build_program()
    res = run_bass_kernel_spmd(nc, in_maps, core_ids=list(range(NC)))
    out = np.empty((4, S, D), np.float32)
    for c in range(NC):
        b_, h_ = c // 2, c % 2
        out[b_, h_ * T:(h_ + 1) * T, :] = res.results[c]["xout"].T
    return out
